# version 16
# baseline (speedup 1.0000x reference)
import numpy as np
import ml_dtypes
from contextlib import ExitStack

import concourse.bass as bass
import concourse.mybir as mybir
from concourse.bass_utils import run_bass_kernel_spmd

F32 = mybir.dt.float32
BF16 = mybir.dt.bfloat16
F16 = mybir.dt.float16
FP8 = mybir.dt.float8e4
AF = mybir.ActivationFunctionType
PM = mybir.MatmulPerfMode
ALU = mybir.AluOpType
AXX = mybir.AxisListType.X

D_MODEL = 1024
SEQ = 4096
HALF = 2048
NH_LOC = 8
HID = 4096
NSTATE = 256
NEG_BIG = -30000.0

ENGS = ("sync", "tensor", "vector", "scalar", "gpsimd")


class Tok:
    __slots__ = ("eng", "rec")

    def __init__(self, eng, rec):
        self.eng = eng
        self.rec = rec


NDMASLOT = 4


class Sched:

    def __init__(self):
        self.ops = {e: [] for e in ENGS}
        self.dma_idx = {e: 0 for e in ENGS}
        self.dma_prev = {e: [None] * NDMASLOT for e in ENGS}

    def add(self, eng, fn, waits=(), dma=False, own_sem=False, hard=()):
        rec = {"fn": fn, "waits": [], "inc": 16 if dma else 1,
               "need": bool(dma), "own": own_sem, "val": None, "sem": None,
               "slot": None}
        if dma and not own_sem:
            slot = self.dma_idx[eng] % NDMASLOT
            self.dma_idx[eng] += 1
            rec["slot"] = slot
            prev = self.dma_prev[eng][slot]
            if prev is not None:
                prev.rec["need"] = True
                rec["waits"].append(prev)
        for w in list(waits) + list(hard):
            if w is None:
                continue
            w.rec["need"] = True
            rec["waits"].append(w)
        tok = Tok(eng, rec)
        if dma and not own_sem:
            self.dma_prev[eng][rec["slot"]] = tok
        self.ops[eng].append(rec)
        return tok

    def emit(self, nc, block, stack):
        sems = {}
        dsems = {}
        own_i = [0]
        for e in ENGS:
            sems[e] = stack.enter_context(nc.semaphore(f"sem_{e}"))
            if self.dma_idx[e]:
                dsems[e] = [stack.enter_context(nc.semaphore(f"dsem_{e}{i}"))
                            for i in range(NDMASLOT)]
            cnt = 0
            dcnt = [0] * NDMASLOT
            for rec in self.ops[e]:
                if rec["own"]:
                    own_i[0] += 1
                    rec["sem"] = stack.enter_context(
                        nc.semaphore(f"sem_own{own_i[0]}"))
                    rec["val"] = rec["inc"]
                elif rec["slot"] is not None:
                    s = rec["slot"]
                    dcnt[s] += 16
                    rec["sem"] = dsems[e][s]
                    rec["val"] = dcnt[s]
                elif rec["need"]:
                    cnt += rec["inc"]
                    rec["sem"] = sems[e]
                    rec["val"] = cnt

        def section(engine, e):
            waited = {}
            for rec in self.ops[e]:
                for w in rec["waits"]:
                    wr = w.rec
                    key = id(wr["sem"])
                    if waited.get(key, -1) >= wr["val"]:
                        continue
                    engine.wait_ge(wr["sem"], wr["val"])
                    waited[key] = wr["val"]
                inst = rec["fn"](engine)
                if rec["need"] or rec["own"]:
                    inst.then_inc(rec["sem"], rec["inc"])

        @block.sync
        def _(sync):
            section(sync, "sync")

        @block.tensor
        def _(tensor):
            section(tensor, "tensor")

        @block.vector
        def _(vector):
            section(vector, "vector")

        @block.scalar
        def _(scalar):
            section(scalar, "scalar")

        @block.gpsimd
        def _(gpsimd):
            section(gpsimd, "gpsimd")


def last(*insts):
    return insts[-1]


def build_program(stop_after="full", debug=(), skip_attn=False):
    nc = bass.Bass("TRN2", target_bir_lowering=False)
    sc = Sched()
    dbg = set(debug)
    ldw_skip = []

    def mark_ldw(inst):
        ldw_skip.append(inst.ins.name)
        return inst

    def apply_ldw_skip():
        import dataclasses
        import os as _os
        if _os.environ.get("BASS_SIM_FP64") == "1" or \
                _os.environ.get("KERN_NO_LDW") == "1":
            return
        names = set(ldw_skip)
        if not names:
            return
        for bb in nc.m.functions[0].blocks:
            insts = list(bb.instructions)
            ch = False
            for i, ins in enumerate(insts):
                if getattr(ins, "name", None) in names:
                    insts[i] = dataclasses.replace(ins, ldweights=False)
                    ch = True
            if ch:
                bb.instructions = insts

    def din(name, shape, dt=F32):
        return nc.dram_tensor(name, shape, dt, kind="ExternalInput").ap()

    x_fm = din("x_fm", [D_MODEL, SEQ])
    xmine_fm = din("xmine_fm", [128, 8 * HALF])
    wqk_in = din("wqk", [D_MODEL, 1024])
    wv_in = din("wv", [D_MODEL, 512])
    wo_in = din("wo", [D_MODEL, D_MODEL])
    w1t_in = din("w1t", [128, 32768])
    w2t_in = din("w2t", [128, 32768])
    b1c = din("b1c", [128, 32])
    b2c = din("b2c", [128, 8])
    bwt_in = din("bwt", [D_MODEL, NSTATE])
    cwt_in = din("cwt", [NSTATE, D_MODEL])
    lamlog = din("lamlog", [128, 2])
    dcol = din("dcol", [128, 8])
    dmask16 = din("dmask16", [D_MODEL, HALF], FP8)
    masknegqk = din("masknegqk", [128, 128])
    keepkq = din("keepkq", [128, 128], BF16)
    mnegkq = din("mnegkq", [128, 128])
    mneg4 = din("mneg4", [128, 512], BF16)
    gA = din("gA", [128, 1])
    gB = din("gB", [128, 1])
    ident_in = din("ident", [128, 128])

    y_out = nc.dram_tensor("y_out", [D_MODEL, HALF], F32, kind="ExternalOutput").ap()

    dbg_outs = {}

    def dout(name, shape, dt):
        dbg_outs[name] = nc.dram_tensor(name, shape, dt, kind="ExternalOutput").ap()
        return dbg_outs[name]

    w1q_d = nc.dram_tensor("w1q_d", [128, 32768], FP8).ap()
    w2q_d = nc.dram_tensor("w2q_d", [128, 32768], FP8).ap()
    mrow_d = nc.dram_tensor("mrow_d", [NH_LOC * 32, 128], BF16).ap()
    og_in = nc.dram_tensor("og_in", [512, SEQ], BF16)
    og_out_c = [nc.dram_tensor(f"og_out_c{i}", [256, SEQ], BF16)
                for i in range(4)]
    carry_in_d = nc.dram_tensor("carry_in_d", [128, 2], F32)
    carry_out_d = nc.dram_tensor("carry_out_d", [256, 2], F32)

    root = ExitStack()
    ep = root.enter_context

    ps = [ep(nc.psum_tensor(f"ps{i}", [128, 512], F32)) for i in range(8)]

    bwt_bf = ep(nc.sbuf_tensor("bwt_bf", [128, 8, 256], BF16))
    cwt_bf = ep(nc.sbuf_tensor("cwt_bf", [128, 2, 1024], BF16))
    b1q = ep(nc.sbuf_tensor("b1q", [128, 32], F32))
    b2q = ep(nc.sbuf_tensor("b2q", [128, 8], F32))
    lam_col = ep(nc.sbuf_tensor("lam_col", [128, 2], F32))
    d_sb = ep(nc.sbuf_tensor("d_sb", [128, 8], F32))
    mneg_sb = ep(nc.sbuf_tensor("mneg_sb", [128, 128], F32))
    mneg4_sb = ep(nc.sbuf_tensor("mneg4_sb", [128, 512], BF16))
    id_sb = ep(nc.sbuf_tensor("id_sb", [128, 128], F32))
    gA_sb = ep(nc.sbuf_tensor("gA_sb", [128, 1], F32))
    gB_sb = ep(nc.sbuf_tensor("gB_sb", [128, 1], F32))
    onesrow_f = ep(nc.sbuf_tensor("onesrow_f", [1, 128], F32))
    ones_col = ep(nc.sbuf_tensor("ones_col", [128, 1], F32))
    murow = ep(nc.sbuf_tensor("murow", [1, 512], F32))
    m2row = ep(nc.sbuf_tensor("m2row", [1, 512], F32))
    varrow = ep(nc.sbuf_tensor("varrow", [1, 512], F32))
    rrow = ep(nc.sbuf_tensor("rrow", [1, 512], F32))
    b1_8 = ep(nc.sbuf_tensor("b1_8", [128, 32], FP8))
    b2_8 = ep(nc.sbuf_tensor("b2_8", [128, 8], FP8))
    sq = [ep(nc.sbuf_tensor(f"sq{i}", [128, 512], F32)) for i in range(4)]

    _small_loads = [
        (b1q, b1c), (b2q, b2c), (lam_col, lamlog), (d_sb, dcol),
        (mneg_sb, masknegqk), (id_sb, ident_in),
        (mneg4_sb, mneg4),
        (gA_sb, gA), (gB_sb, gB),
    ]
    for _dst, _src in _small_loads:
        t_small = sc.add("sync", (lambda d_, s_: lambda s: s.dma_start(
            out=d_[:, :], in_=s_))(_dst, _src), dma=True)

    t_c1 = sc.add("vector", lambda v: last(
        v.memset(onesrow_f[:, :], 1.0),
        v.memset(ones_col[:, :], 1.0),
        v.tensor_copy(out=b1_8[:, :], in_=b1q[:, :]),
        v.tensor_copy(out=b2_8[:, :], in_=b2q[:, :]),
    ), waits=[t_small])
    t_consts = sc.add("vector", lambda v: last(
        v.tensor_copy(out=b1q[:, :], in_=b1_8[:, :]),
        v.tensor_copy(out=b2q[:, :], in_=b2_8[:, :]),
    ), hard=[t_c1])

    t_lam = sc.add("scalar", lambda s: s.activation(
        out=lam_col[:, :], in_=lam_col[:, :], func=AF.Sigmoid), waits=[t_small])

    scopeA = ExitStack()
    ea = scopeA.enter_context
    q_all = ea(nc.sbuf_tensor("q_all", [128, 4, SEQ], BF16))
    k_pack = ea(nc.sbuf_tensor("k_pack", [128, 4, SEQ], BF16))
    v_buf = ea(nc.sbuf_tensor("v_buf", [128, 32, 520], BF16))

    scopeA0 = ExitStack()
    e0 = scopeA0.enter_context
    wqk_bf = e0(nc.sbuf_tensor("wqk_bf", [128, 8, 1024], BF16))
    wv_bf = e0(nc.sbuf_tensor("wv_bf", [128, 8, 512], BF16))
    wstream = [e0(nc.sbuf_tensor(f"wstream{i}", [128, 2048], F32)) for i in range(2)]
    w8buf = [e0(nc.sbuf_tensor(f"w8buf{i}", [128, 2048], FP8)) for i in range(2)]

    last_cast = [None, None]
    last_store = [None, None]

    def stream_weight(src_full, n_tiles, cols, dst_sb=None, dst_dma=None,
                      col_tiles=False):
        toks = []
        for i in range(n_tiles):
            b = i % 2
            if col_tiles:
                src = src_full[:, i * cols:(i + 1) * cols]
            else:
                src = src_full[i * 128:(i + 1) * 128, :]
            t_load = sc.add(
                ("sync", "gpsimd")[i % 2],
                (lambda bb, cc, sp: lambda s: s.dma_start(
                    out=wstream[bb][:, 0:cc], in_=sp))(b, cols, src),
                waits=[last_cast[b]], dma=True)
            if dst_dma is None:
                t_c = sc.add(
                    "vector",
                    (lambda bb, cc, ii: lambda v: v.tensor_copy(
                        out=dst_sb(ii), in_=wstream[bb][:, 0:cc]))(b, cols, i),
                    waits=[t_load])
                last_cast[b] = t_c
                toks.append(t_c)
            else:
                t_c = sc.add(
                    "vector",
                    (lambda bb, cc: lambda v: v.tensor_copy(
                        out=w8buf[bb][:, 0:cc], in_=wstream[bb][:, 0:cc]))(b, cols),
                    waits=[t_load, last_store[b]])
                last_cast[b] = t_c
                t_s = sc.add(
                    ("gpsimd", "sync")[i % 2],
                    (lambda bb, cc, dp: lambda g: g.dma_start(
                        out=dp, in_=w8buf[bb][:, 0:cc]))(
                            b, cols, dst_dma[:, i * cols:(i + 1) * cols]),
                    waits=[t_c], dma=True)
                last_store[b] = t_s
                toks.append(t_s)
        return toks

    tw_qk = stream_weight(wqk_in, 8, 1024, dst_sb=lambda i: wqk_bf[:, i, :])
    tw_v = stream_weight(wv_in, 8, 512, dst_sb=lambda i: wv_bf[:, i, 0:512])
    tw_bw = stream_weight(bwt_in, 8, 256, dst_sb=lambda i: bwt_bf[:, i, 0:256])
    tw_cw = stream_weight(cwt_in, 2, 1024, dst_sb=lambda i: cwt_bf[:, i, :])
    tw_1 = stream_weight(w1t_in, 16, 2048, dst_dma=w1q_d, col_tiles=True)
    tw_2 = stream_weight(w2t_in, 16, 2048, dst_dma=w2q_d, col_tiles=True)
    t_wdone = ([tw_qk[-1], tw_v[-1], tw_bw[-1], tw_cw[-1], t_consts]
               + tw_1[-4:] + tw_2[-4:])

    CH = 256
    xs = [e0(nc.sbuf_tensor(f"xs{i}", [128, 8, CH], F32)) for i in range(2)]
    xln = e0(nc.sbuf_tensor("xln", [128, 8, 2 * CH], BF16))

    ln_mm_prev = [None]
    ln_bc_prev = [None]
    ln_sd_prev = [None]
    ln_norm_done = [None, None]
    ln_norm_chain = [None]
    qkv_war = [None]
    dummy_gate = [None]

    def emit_ln(width, src_ap_fn, dst_write, waits_src, pha):
        bmu, br = (5, 6) if pha == 0 else (4, 7)
        t_mu = None
        for dt in range(8):
            t_mu = sc.add("tensor", (lambda d, first: lambda t: t.matmul(
                ps[bmu][0:1, 0:width], ones_col[:, :], src_ap_fn(d),
                start=first, stop=(d == 7)))(dt, dt == 0),
                waits=[t_mu, ln_norm_done[pha] if dt == 0 else None]
                + list(waits_src))
        t_mm = None
        sqmm_toks = []
        for dt in range(8):
            t_sq_d = sc.add("scalar", (lambda d, ss: lambda s: s.activation(
                out=sq[ss][:, 0:width], in_=src_ap_fn(d), func=AF.Square))(
                    dt, dt % 4),
                waits=list(waits_src)
                + [ln_mm_prev[0] if dt < 4 else sqmm_toks[dt - 4]])
            t_mm = sc.add("tensor", (lambda d, ss, first: lambda t: t.matmul(
                ps[br][0:1, 0:width], ones_col[:, :], sq[ss][:, 0:width],
                start=first, stop=(d == 7)))(dt, dt % 4, dt == 0),
                waits=[t_sq_d, t_mm,
                       ln_norm_done[pha] if dt == 0 else None])
            sqmm_toks.append(t_mm)
        ln_mm_prev[0] = t_mm
        t_st1 = sc.add("vector", (lambda w: lambda v: last(
            v.tensor_scalar_mul(out=murow[:, 0:w], in0=ps[bmu][0:1, 0:w],
                                scalar1=1.0 / 1024.0),
            v.tensor_scalar_mul(out=m2row[:, 0:w], in0=ps[br][0:1, 0:w],
                                scalar1=1.0 / 1024.0),
        ))(width), waits=[t_mm, t_mu, ln_bc_prev[0]])
        t_st2 = sc.add("vector", (lambda w: lambda v: v.tensor_tensor(
            out=varrow[:, 0:w], in0=murow[:, 0:w],
            in1=murow[:, 0:w], op=ALU.mult))(width), hard=[t_st1],
            waits=[ln_sd_prev[0]])
        t_st3 = sc.add("vector", (lambda w: lambda v: v.tensor_tensor(
            out=varrow[:, 0:w], in0=m2row[:, 0:w],
            in1=varrow[:, 0:w], op=ALU.subtract))(width), hard=[t_st2])
        t_stat = sc.add("vector", (lambda w: lambda v: v.tensor_scalar_add(
            out=varrow[:, 0:w], in0=varrow[:, 0:w], scalar1=1e-5))(width),
            hard=[t_st3])
        t_sd = sc.add("scalar", (lambda w: lambda s: s.activation(
            out=varrow[:, 0:w], in_=varrow[:, 0:w], func=AF.Sqrt))(width),
            waits=[t_stat])
        ln_sd_prev[0] = t_sd
        t_r = sc.add("vector", (lambda w: lambda v: v.reciprocal(
            out=rrow[:, 0:w], in_=varrow[:, 0:w]))(width), waits=[t_sd])
        t_bc = sc.add("tensor", (lambda w: lambda t: last(
            t.matmul(ps[bmu][:, 0:w], onesrow_f[:, :], murow[:, 0:w],
                     start=True, stop=True),
            t.matmul(ps[br][:, 0:w], onesrow_f[:, :], rrow[:, 0:w],
                     start=True, stop=True),
        ))(width), waits=[t_r, t_stat, dummy_gate[0]])
        ln_bc_prev[0] = t_bc
        t_norm = []
        tprev = ln_norm_chain[0]
        for dt in range(8):
            fns = dst_write(None, dt, bmu, br)
            tn = None
            for f in fns:
                tn = sc.add("vector", f,
                            waits=[t_bc, qkv_war[0] if tn is None else None],
                            hard=[tn if tn is not None else tprev])
            tprev = tn
            t_norm.append(tn)
        ln_norm_chain[0] = t_norm[-1]
        ln_norm_done[pha] = t_norm[-1]
        return t_norm

    t_p1_parts = []
    prev_xs = [None, None]
    qkv_pend = [None]
    qkv_pend2 = [None]
    emit_qkv_ref = [None]
    NC1 = SEQ // CH
    last_bank = {}
    for c in ([] if skip_attn else range(NC1)):
        b = c % 2
        c0 = c * CH
        t_load = [sc.add(("sync", "gpsimd", "sync")[si],
            (lambda bb, a0, t0, t1: lambda s: s.dma_start(
                out=xs[bb][:, t0:t1, :],
                in_=x_fm[t0 * 128:t1 * 128, a0:a0 + CH].rearrange(
                    "(t p) n -> p t n", p=128)))(b, c0, lo, hi),
            waits=[prev_xs[b]] + (t_wdone if c == 0 else []), dma=True)
            for si, (lo, hi) in enumerate(((0, 3), (3, 6), (6, 8)))]

        def dstp1(v, d, bmu, br, _b=b, _c=c):
            return [
                lambda v2: v2.tensor_tensor(out=xs[_b][:, d, :], in0=xs[_b][:, d, :],
                                            in1=ps[bmu][:, 0:CH], op=ALU.subtract),
                lambda v2: v2.tensor_tensor(
                    out=xln[:, d, (_c % 2) * CH:(_c % 2) * CH + CH],
                    in0=xs[_b][:, d, :],
                    in1=ps[br][:, 0:CH], op=ALU.mult),
            ]
        if qkv_pend2[0] is not None:
            emit_qkv_ref[0](*qkv_pend2[0])
            qkv_pend2[0] = None
        t_norm = emit_ln(CH, (lambda _b=b: lambda d: xs[_b][:, d, :])(),
                         dstp1, t_load, c % 2)
        prev_xs[b] = t_norm[-1]

        def emit_qkv(c, t_norm2):
            c0p = (c - 1) * CH
            chunk_last = None
            for j in range(NH_LOC):
                bank = j % 4
                r0 = 64 * (j % 2)
                t_mmp = sc.add("tensor", (lambda jj, bk: lambda t: [
                    t.matmul(ps[bk][:, 0:2 * CH],
                             wqk_bf[:, d, jj * 128:(jj + 1) * 128],
                             xln[:, d, :], start=(d == 0), stop=(d == 7))
                    for d in range(8)][-1])(j, bank),
                    waits=t_norm2 + [last_bank.get(bank)])
                t_ev = sc.add("vector", (lambda jj, bk, a0, rr: lambda v: last(
                    v.tensor_copy(out=q_all[rr:rr + 64, jj // 2, a0:a0 + 2 * CH],
                                  in_=ps[bk][0:64, 0:2 * CH]),
                    v.tensor_copy(out=k_pack[rr:rr + 64, jj // 2, a0:a0 + 2 * CH],
                                  in_=ps[bk][64:128, 0:2 * CH]),
                ))(j, bank, c0p, r0), waits=[t_mmp])
                last_bank[bank] = t_ev
                chunk_last = t_ev
            for ti in range(4):
                tt_g = (c - 1) * 2 + ti
                bank = ti % 4
                t_mmp = sc.add("tensor", (lambda tl, bk: lambda t: [
                    t.matmul(ps[bk][:, :], xln[:, d, tl * 128:(tl + 1) * 128],
                             wv_bf[:, d, 0:512], start=(d == 0), stop=(d == 7))
                    for d in range(8)][-1])(ti, bank),
                    waits=t_norm2 + [last_bank.get(bank)])
                t_ev = sc.add("vector", (lambda tg, bk: lambda v: last(
                    v.tensor_copy(
                        out=v_buf[:, tg, :].rearrange("p (h d) -> p h d", h=8)[:, :, 0:64],
                        in_=ps[bk][:, :].rearrange("p (h d) -> p h d", h=8)),
                    v.memset(v_buf[:, tg, :].rearrange("p (h d) -> p h d", h=8)[:, :, 64:65],
                             1.0),
                ))(tt_g, bank), waits=[t_mmp])
                last_bank[bank] = t_ev
                chunk_last = t_ev
            qkv_war[0] = chunk_last
            t_p1_parts.append(chunk_last)

        emit_qkv_ref[0] = emit_qkv
        if c % 2 == 0:
            qkv_pend[0] = t_norm
        else:
            qkv_pend2[0] = (c, qkv_pend[0] + t_norm)
    if qkv_pend2[0] is not None:
        emit_qkv_ref[0](*qkv_pend2[0])
        qkv_pend2[0] = None

    t_p1_done = t_p1_parts[-1] if t_p1_parts else t_wdone[-1]
    scopeA0.close()

    if "qkv" in dbg:
        tq = dout("dbg_q", [128, 4, SEQ], BF16)
        tk = dout("dbg_k", [128, 4, SEQ], BF16)
        tv = dout("dbg_v", [128, 32, 520], BF16)
        sc.add("gpsimd", lambda g: g.dma_start(out=tq, in_=q_all[:, :, :]),
               waits=t_p1_parts, dma=True)
        sc.add("gpsimd", lambda g: g.dma_start(out=tk, in_=k_pack[:, :, :]),
               waits=t_p1_parts, dma=True)
        t_p1_done = sc.add("gpsimd", lambda g: g.dma_start(
            out=tv, in_=v_buf[:, :, :]), waits=t_p1_parts, dma=True)

    scopeA2 = ExitStack()
    e2 = scopeA2.enter_context
    o_cat = e2(nc.sbuf_tensor("o_cat", [128, 4, SEQ], BF16))
    ktil = [e2(nc.sbuf_tensor(f"ktil{i}", [65, SEQ], BF16)) for i in range(2)]
    qhat = [e2(nc.sbuf_tensor(f"qhat{i}", [65, 32, 128], BF16)) for i in range(2)]
    pbuf = [e2(nc.sbuf_tensor(f"pbuf{i}", [128, 1024], BF16)) for i in range(2)]
    mcol2 = [e2(nc.sbuf_tensor(f"mcol2_{i}", [128, 32], F32)) for i in range(2)]
    mtmp = e2(nc.sbuf_tensor("mtmp", [128, 1], F32))
    mtmp2 = e2(nc.sbuf_tensor("mtmp2", [128, 1], F32))
    dsc = e2(nc.sbuf_tensor("dsc", [128, 128], F32))
    mrow_sb = [e2(nc.sbuf_tensor(f"mrow_sb{i}", [32, 128], BF16)) for i in range(2)]
    zrow = e2(nc.sbuf_tensor("zrow", [1, 512], F32))
    zb = e2(nc.sbuf_tensor("zb", [64, 512], F32))

    t_ktA = sc.add("vector", lambda v: last(
        v.memset(ktil[0][64:65, :], 1.0),
        v.memset(ktil[1][64:65, :], 1.0),
    ), waits=[t_p1_done] + t_wdone)

    rot = [None, None]
    g_ctr = [0]
    dummy_prev = [None]

    def emit_dummy():
        dummy_prev[0] = sc.add("tensor", lambda t: t.matmul(
            ps[4][64:65, 0:512], ktil[0][64:65, 0:1], ktil[0][64:65, 0:512],
            start=True, stop=True, skip_group_check=True),
            waits=[dummy_prev[0], t_ktA])
    kq_load = [None, None]
    kq_free = [None, None]
    mcol_free = [None, None]
    mrow_free = [None, None]
    ps4_free = [None]
    red_chain = [None]
    pool_head = {}
    mrst = {}

    def emit_red(aa, ww, has_d, bb, qtix, t_mm_tok):
        tok = red_chain[0]
        wnd = ww - 128 if has_d else ww
        segs = []
        if wnd > 0:
            segs.append(("r", ps[2 * aa][:, 0:min(512, wnd)]))
            if wnd > 512:
                segs.append(("r", ps[2 * aa + 1][:, 0:wnd - 512]))
        if has_d:
            doff = ww - 128
            bank_ap = (ps[2 * aa][:, doff:doff + 128] if doff < 512
                       else ps[2 * aa + 1][:, doff - 512:doff - 512 + 128])
            segs.append(("d", bank_ap))
        for si, (kind, ap) in enumerate(segs):
            mt = mtmp if si % 2 == 0 else mtmp2
            if kind == "r":
                t_r = sc.add("vector", (lambda a_, m_: lambda v:
                             v.reduce_max(out=m_[:, :], in_=a_,
                                          axis=AXX))(ap, mt),
                             waits=[t_mm_tok, tok])
            else:
                t_d = sc.add("vector", (lambda a_: lambda v:
                             v.tensor_tensor(out=dsc[:, :], in0=a_,
                                             in1=mneg_sb[:, :],
                                             op=ALU.add))(ap),
                             waits=[t_mm_tok, tok])
                t_r = sc.add("vector", (lambda m_: lambda v:
                             v.reduce_max(out=m_[:, :], in_=dsc[:, :],
                                          axis=AXX))(mt), hard=[t_d])
            tok = sc.add("vector", (lambda m_, b_, qi: lambda v:
                         v.tensor_tensor(
                             out=mcol2[b_][:, qi:qi + 1],
                             in0=mcol2[b_][:, qi:qi + 1],
                             in1=m_[:, :], op=ALU.max))(mt, bb, qtix),
                         hard=[t_r])
        red_chain[0] = tok
        return tok

    def gen_A(j):
        b = j % 2
        t_kb = sc.add("vector", (lambda bb, rr, hh: lambda v: last(
            v.tensor_copy(out=ktil[bb][0:64, :], in_=k_pack[rr:rr + 64, hh, :]),
            v.tensor_copy(
                out=qhat[bb][0:64, :, :],
                in_=q_all[rr:rr + 64, hh, :].rearrange("p (a b) -> p a b", b=128)),
        ))(b, 64 * (j % 2), j // 2), waits=[t_ktA, kq_free[b]])
        kq_load[b] = t_kb
        red_chain[0] = sc.add("vector", (lambda bb: lambda v: v.memset(
            mcol2[bb][:, :], -3.0e38))(b),
            waits=[mcol_free[b], red_chain[0]])
        for qt in range(32):
            ntile = qt + 1
            g0 = 0
            while g0 < ntile:
                gw = min(8, ntile - g0)
                a = g_ctr[0] % 2
                g_ctr[0] += 1
                has_diag = (g0 + gw == ntile)
                wcols = gw * 128
                t_mmA = sc.add("tensor", (lambda bb, q, k0, ww, aa: lambda t: [
                    (mark_ldw if off > 0 else (lambda z: z))(
                        t.matmul(ps[2 * aa + (off // 512)][:, off % 512:off % 512 + min(512, ww - off)],
                                 qhat[bb][0:64, q, :],
                                 ktil[bb][0:64, k0 * 128 + off:k0 * 128 + off + min(512, ww - off)],
                                 start=True, stop=True))
                    for off in range(0, ww, 512)][-1])(b, qt, g0, wcols, a),
                    waits=[t_kb, rot[a]])
                rot[a] = emit_red(a, wcols, has_diag, b, qt, t_mmA)
                emit_dummy()
                g0 += gw
                yield
        pool_head[j] = red_chain[0]

    def emit_T(j):
        b = j % 2
        t_tr = sc.add("tensor", (lambda bb: lambda t: t.transpose(
            ps[4][0:32, 0:128], mcol2[bb][:, :], id_sb[:, :]))(b),
            waits=[pool_head[j], ps4_free[0]])
        mcol_free[b] = t_tr
        t_ng = sc.add("vector", (lambda bb: lambda v: v.tensor_scalar_mul(
            out=mrow_sb[bb][:, :], in0=ps[4][0:32, 0:128], scalar1=-1.0))(b),
            waits=[t_tr, mrow_free[b]])
        ps4_free[0] = t_ng
        t_st = sc.add("sync", (lambda bb, jj: lambda s: s.dma_start(
            out=mrow_d[jj * 32:(jj + 1) * 32, :], in_=mrow_sb[bb][:, :]))(b, j),
            waits=[t_ng], dma=True)
        mrst[j] = t_st
        mrow_free[b] = t_st

    pfree = [None, None]
    evac_hist = []
    prev_bcz = [None]
    prev_zb = [None]
    prev_evac = [None]
    pend = [None]
    zpend = [None]
    t_attn_last = []
    t_og_chunks = []
    R_ctr = [0]

    def flush_z():
        if zpend[0] is None:
            return
        R, jj, t_av = zpend[0]
        zpend[0] = None
        B = 6 + (R % 2)
        rloc = R % 8
        qt0 = rloc * 4
        hh = jj // 2
        rr = 64 * (jj % 2)
        t_rz = sc.add("vector", (lambda b_: lambda v: v.reciprocal(
            out=zrow[0:1, 0:512], in_=ps[b_][64:65, 0:512]))(B),
            waits=[t_av, prev_bcz[0]])
        t_bcz = sc.add("tensor", lambda t: t.matmul(
            ps[5][0:64, 0:512], onesrow_f[0:1, 0:64], zrow[0:1, 0:512],
            start=True, stop=True), waits=[t_rz, prev_zb[0]])
        prev_bcz[0] = t_bcz
        t_zb = sc.add("vector", lambda v: v.tensor_copy(
            out=zb[:, 0:512], in_=ps[5][0:64, 0:512]),
            waits=[t_bcz, prev_evac[0]])
        prev_zb[0] = t_zb
        t_evac = sc.add("vector", (lambda b_, q0, h_, r_: lambda v:
            v.tensor_tensor(out=o_cat[r_:r_ + 64, h_, q0 * 128:q0 * 128 + 512],
                            in0=ps[b_][0:64, 0:512], in1=zb[:, 0:512],
                            op=ALU.mult))(B, qt0, hh, rr), hard=[t_zb])
        prev_evac[0] = t_evac
        evac_hist.append(t_evac)
        if rloc == 7:
            t_attn_last.append(t_evac)
            if jj % 2 == 1:
                t_st8 = sc.add("gpsimd", (lambda h_: lambda g: g.dma_start(
                    out=og_in.ap()[h_ * 128:(h_ + 1) * 128, :],
                    in_=o_cat[:, h_, :]))(hh),
                    waits=[t_evac, t_attn_last[jj - 1]], dma=True)
                t_cc = sc.add("gpsimd", (lambda h_: lambda g: g.collective_compute(
                    "AllGather", ALU.bypass,
                    replica_groups=[[0, 1], [2, 3], [4, 5], [6, 7]],
                    ins=[og_in[h_ * 128:(h_ + 1) * 128, :]],
                    outs=[og_out_c[h_][:, :]]))(hh),
                    waits=[t_st8], own_sem=True)
                t_og_chunks.append(t_cc)

    def flush_pend():
        if pend[0] is None:
            flush_z()
            return
        jj, qp, k0, gw, pb, t_e, first, last = pend[0]
        pend[0] = None
        flush_z()
        R = R_ctr[0]
        B = 6 + (R % 2)
        paircol = (qp % 2) * 256
        gate = None
        if first and qp % 2 == 0:
            gate = evac_hist[R - 2] if R >= 2 else None
        t_av = sc.add("tensor", (lambda jj_, k0_, gw_, pb_, pc_, b_, f_, l_:
            lambda t: [
                t.matmul(ps[b_][0:65, pc_:pc_ + 256],
                         v_buf[:, k0_ + i2, 65 * jj_:65 * jj_ + 65],
                         pbuf[pb_][:, i2 * 256:(i2 + 1) * 256],
                         start=(f_ and i2 == 0), stop=(l_ and i2 == gw_ - 1),
                         skip_group_check=True)
                for i2 in range(gw_)][-1])(jj, k0, gw, pb, paircol, B, first, last),
            waits=[t_e, gate])
        pfree[pb] = t_av
        if last and qp % 2 == 1:
            zpend[0] = (R, jj, t_av)
            R_ctr[0] += 1

    def gen_B(j):
        jb = j % 2
        t_qrow = sc.add("sync", (lambda bb, jj: lambda s: s.dma_start(
            out=qhat[bb][64:65, :, :],
            in_=mrow_d[jj * 32:(jj + 1) * 32, :].rearrange(
                "a b -> (a b)")[None, :].rearrange("o (a b) -> o a b", b=128)))(jb, j),
            waits=[mrst[j], kq_load[jb]], dma=True)
        t_klast = None
        for qp in range(16):
            ntile = 2 * qp + 2
            k0 = 0
            while k0 < ntile:
                gw = min(4, ntile - k0)
                a = g_ctr[0] % 2
                pb = g_ctr[0] % 2
                g_ctr[0] += 1
                t_qb = sc.add("tensor", (lambda bb, q2, k0_, gw_, aa: lambda t: [
                    t.matmul(ps[2 * aa + (i2 // 2)][:, (i2 % 2) * 256:(i2 % 2) * 256 + 256],
                             ktil[bb][0:65, (k0_ + i2) * 128:(k0_ + i2 + 1) * 128],
                             qhat[bb][0:65, q2:q2 + 2, :].rearrange("p a b -> p (a b)"),
                             start=True, stop=True, skip_group_check=True)
                    for i2 in range(gw_)][-1])(jb, 2 * qp, k0, gw, a),
                    waits=[t_qrow, rot[a]])
                t_klast = t_qb
                flush_pend()
                is_last = (k0 + gw == ntile)
                t_m = None
                if is_last:
                    moff = (ntile - 2 - k0) * 256
                    t_m = sc.add("vector", (lambda aa, mo: lambda v:
                        v.tensor_tensor(
                            out=ps[2 * aa + (mo // 512)][:, 0:512],
                            in0=ps[2 * aa + (mo // 512)][:, 0:512],
                            in1=mneg4_sb[:, :], op=ALU.add))(a, moff),
                        waits=[t_qb])
                t_e = sc.add("scalar", (lambda aa, ww, pp: lambda s: last(
                    s.activation(out=pbuf[pp][:, 0:512],
                                 in_=ps[2 * aa][:, 0:512],
                                 func=AF.Exp, scale=8.0),
                    *([s.activation(out=pbuf[pp][:, 512:1024],
                                    in_=ps[2 * aa + 1][:, 0:512],
                                    func=AF.Exp, scale=8.0)] if ww > 512 else []),
                ))(a, gw * 256, pb), waits=[t_qb, t_m, pfree[pb]])
                rot[a] = t_e
                pend[0] = (j, qp, k0, gw, pb, t_e, k0 == 0, is_last)
                emit_dummy()
                k0 += gw
                yield
        kq_free[jb] = t_klast

    if not skip_attn:
        ga = gen_A(0)
        for _ in ga:
            pass
        emit_T(0)
        BURST = 1
        for j in range(NH_LOC):
            gb = gen_B(j)
            ga = gen_A(j + 1) if j + 1 < NH_LOC else None
            done_b = False
            while not done_b:
                for _ in range(BURST):
                    if ga is None:
                        break
                    try:
                        next(ga)
                    except StopIteration:
                        ga = None
                        emit_T(j + 1)
                for _ in range(BURST):
                    try:
                        next(gb)
                    except StopIteration:
                        done_b = True
                        break
            while ga is not None:
                try:
                    next(ga)
                except StopIteration:
                    ga = None
                    emit_T(j + 1)
        flush_pend()
        flush_z()
        dummy_gate[0] = dummy_prev[0]

    t_p2_done = t_attn_last[-1] if t_attn_last else t_ktA

    if "ocat" in dbg:
        toc = dout("dbg_ocat", [128, 4, SEQ], BF16)
        t_p2_done = sc.add("gpsimd", lambda g: g.dma_start(
            out=toc, in_=o_cat[:, :, :]), waits=t_attn_last, dma=True)

    if not skip_attn:
        t_og_st = t_og_chunks[-1]
        t_cc1 = t_og_chunks[-1]
    else:
        t_og_st = t_p2_done
        t_cc1 = t_p2_done

    scopeA2.close()
    scopeA.close()

    if stop_after == "p2":
        t_fin = sc.add("gpsimd", lambda g: g.dma_start(
            out=y_out, in_=xmine_fm), waits=[t_cc1, t_p2_done], dma=True)
        sc.add("gpsimd", lambda g: g.engine_nop(), waits=[t_fin])
        block = root.enter_context(nc.Block())
        sc.emit(nc, block, root)
        apply_ldw_skip()
        root.close()
        return nc

    junction = [t_cc1, t_p2_done, t_og_st]

    scopeB = ExitStack()
    eb_ = scopeB.enter_context
    resid = eb_(nc.sbuf_tensor("resid", [128, 8, HALF], F32))
    xc_s = eb_(nc.sbuf_tensor("xc_s", [128, 512], F32))
    xf_s = eb_(nc.sbuf_tensor("xf_s", [128, 512], F32))
    md_s = eb_(nc.sbuf_tensor("md_s", [128, 512], F32))
    md2_s = [eb_(nc.sbuf_tensor(f"md2_s{i}", [128, 512], F32)) for i in range(2)]

    if "yb0" in dbg:
        y_cap = eb_(nc.sbuf_tensor("y_cap", [128, 512], F32))
        mk_cap = eb_(nc.sbuf_tensor("mk_cap", [128, 512], F32))
        ps_cap2 = eb_(nc.sbuf_tensor("ps_cap2", [128, 512], F32))

    t_resid0_l = [sc.add(("sync", "gpsimd", "scalar")[si],
        (lambda t0, t1: lambda s: s.dma_start(
            out=resid[:, t0:t1, :],
            in_=xmine_fm[:, t0 * HALF:t1 * HALF].rearrange(
                "p (t n) -> p t n", n=HALF)))(lo, hi),
        waits=junction, dma=True)
        for si, (lo, hi) in enumerate(((0, 3), (3, 6), (6, 8)))]

    scopeB1 = ExitStack()
    e1 = scopeB1.enter_context
    o_my = e1(nc.sbuf_tensor("o_my", [128, 8, HALF], BF16))
    ogt = [e1(nc.sbuf_tensor(f"ogt{i}", [128, 2, HALF], BF16)) for i in range(2)]
    wo_bf = e1(nc.sbuf_tensor("wo_bf", [128, 8, 1024], BF16))
    wos = [e1(nc.sbuf_tensor(f"wos{i}", [128, 1024], F32)) for i in range(2)]

    two_last = [None, None]
    tw_o = []
    for i in ([] if skip_attn else range(8)):
        b = i % 2
        t_l = sc.add(("sync", "gpsimd", "scalar")[i % 3],
            (lambda bb, ii: lambda s: s.dma_start(
                out=wos[bb][:, :], in_=wo_in[ii * 128:(ii + 1) * 128, :]))(b, i),
            waits=[two_last[b]] + junction, dma=True)
        t_c = sc.add("vector", (lambda bb, ii: lambda v: v.tensor_copy(
            out=wo_bf[:, ii, :], in_=wos[bb][:, :]))(b, i), waits=[t_l] + junction)
        two_last[b] = t_c
        tw_o.append(t_c)

    t_omy = []
    prev_og = [None, None]
    for it in ([] if skip_attn else range(8)):
        b = it % 2
        t_g = sc.add(("sync", "gpsimd", "scalar")[it % 3],
            (lambda ii, bb: lambda s: s.dma_start(
                out=ogt[bb][:, :, :],
                in_=og_out_c[ii % 4].ap()[
                    (ii // 4) * 128:(ii // 4) * 128 + 128, :].rearrange(
                    "p (a n) -> p a n", a=2)))(it, b),
            waits=[t_og_chunks[it % 4] if t_og_chunks else None,
                   prev_og[b]] + junction, dma=True)
        t_ca = sc.add("vector", (lambda ii, bb: lambda v: v.tensor_scalar(
            out=o_my[:, ii, :], in0=ogt[bb][:, 0, :],
            scalar1=gA_sb[:, 0:1], scalar2=None, op0=ALU.mult))(it, b),
            waits=[t_g] + junction)
        t_c = sc.add("vector", (lambda ii, bb: lambda v: v.scalar_tensor_tensor(
            out=o_my[:, ii, :], in0=ogt[bb][:, 1, :],
            scalar=gB_sb[:, 0:1], in1=o_my[:, ii, :],
            op0=ALU.mult, op1=ALU.add))(it, b), hard=[t_ca])
        prev_og[b] = t_c
        t_omy.append(t_c)

    t_ffn_in = []
    evac_hist = []
    prev_mmo = [None, None]
    for ct in ([] if skip_attn else range(8)):
        bs = 4 * (ct % 2)
        t_mmp = sc.add("tensor", (lambda cc, bss: lambda t: [
            (mark_ldw if t4 > 0 else (lambda z: z))(
                t.matmul(ps[bss + t4][:, :], wo_bf[:, i2, cc * 128:(cc + 1) * 128],
                         o_my[:, i2, t4 * 512:(t4 + 1) * 512],
                         start=(i2 == 0), stop=(i2 == 7),
                         skip_group_check=True))
            for i2 in range(8) for t4 in range(4)][-1])(ct, bs),
            waits=t_omy + tw_o + [prev_mmo[ct % 2],
                                  evac_hist[ct - 2] if ct >= 2 else None])
        prev_mmo[ct % 2] = t_mmp
        t_last_ct = None
        for tc in range(4):
            t_ev = sc.add("vector", (lambda cc, t4, bss: lambda v: v.tensor_tensor(
                out=resid[:, cc, t4 * 512:(t4 + 1) * 512],
                in0=resid[:, cc, t4 * 512:(t4 + 1) * 512],
                in1=ps[bss + t4][:, :], op=ALU.add))(ct, tc, bs),
                waits=[t_mmp] + t_resid0_l)
            t_last_ct = t_ev
        evac_hist.append(t_last_ct)
        t_ffn_in.append(t_last_ct)

    if skip_attn:
        t_ffn_in = t_resid0_l
    scopeB1.close()

    if "resid1" in dbg:
        tr1 = dout("dbg_resid1", [128, 8, HALF], F32)
        t_ffn_in = t_ffn_in + [sc.add("gpsimd", lambda g: g.dma_start(
            out=tr1, in_=resid[:, :, :]), waits=t_ffn_in[:], dma=True)]

    def emit_ln_quant(t_ready, xq):
        toks = []
        t_prev = None
        for ch in range(4):
            c0 = ch * 512

            def dstq(v, d, bmu, br, _c0=c0):
                return [
                    lambda v2: v2.tensor_tensor(out=xc_s[:, :],
                                                in0=resid[:, d, _c0:_c0 + 512],
                                                in1=ps[bmu][:, :], op=ALU.subtract),
                    lambda v2: v2.tensor_tensor(out=xf_s[:, :], in0=xc_s[:, :],
                                                in1=ps[br][:, :], op=ALU.mult),
                    lambda v2: v2.tensor_copy(out=xq[:, d, _c0:_c0 + 512],
                                              in_=xf_s[:, :]),
                ]
            t_norm = emit_ln(512, (lambda _c0=c0: lambda d: resid[:, d, _c0:_c0 + 512])(),
                             dstq, t_ready, ch % 2)
            toks.extend(t_norm)
        return toks

    def emit_ffn(t_xq_toks, bufs, early_tap=False):
        xq, h_buf, w1s, w2s, h16, mk16 = bufs
        t_h = []
        prev_w1 = [None, None]
        prev_mm = [None, None]
        act_hist = []
        prev_cp = [None, None, None, None]
        pend = [None]
        last_act = [None]
        for ht2 in range(32):
            b = ht2 % 2
            bs = 4 * (ht2 % 2)
            t_w = [sc.add(("sync", "gpsimd")[wi % 2],
                (lambda hh, bb, t0, t1: lambda s: s.dma_start(
                    out=w1s[bb][:, t0:t1, :],
                    in_=w1q_d[:, hh * 1024 + t0 * 128:hh * 1024 + t1 * 128]
                    .rearrange("p (t n) -> p t n", n=128)))(ht2, b, 4 * wi, 4 * wi + 4),
                waits=[prev_w1[b], t_xq_toks[0]], dma=True)
                for wi in range(2)]
            t_mmp = sc.add("tensor", (lambda hh, bb, bss: lambda t: [
                t.matmul(ps[bss + t4][:, :], w1s[bb][:, 2 * d2:2 * d2 + 2, :],
                         xq[:, 2 * d2:2 * d2 + 2, t4 * 512:(t4 + 1) * 512],
                         start=(d2 == 0), stop=(d2 == 3),
                         skip_group_check=True, perf_mode=PM.DoubleRow)
                for d2 in range(4) for t4 in range(4)][-1])(ht2, b, bs),
                waits=t_xq_toks + t_w + [act_hist[ht2 - 2] if ht2 >= 2 else None,
                                         prev_mm[b]])
            prev_w1[b] = t_mmp
            prev_mm[b] = t_mmp
            for tc in range(4):
                t_a = sc.add("scalar", (lambda hh, bss, t4: lambda s: s.activation(
                    out=h16[t4][:, :], in_=ps[bss + t4][:, :], func=AF.Relu,
                    bias=b1q[:, hh:hh + 1]))(ht2, bs, tc),
                    waits=[t_mmp, prev_cp[tc]])
                if pend[0] is not None:
                    p_hh, p_t4 = pend[0]
                    t_c = sc.add("vector", (lambda hh, t4: lambda v: v.tensor_copy(
                        out=h_buf[:, hh, t4 * 512:(t4 + 1) * 512],
                        in_=h16[t4][:, :]))(p_hh, p_t4), waits=[t_a])
                    prev_cp[p_t4] = t_c
                    t_h.append(t_c)
                pend[0] = (ht2, tc)
                last_act[0] = t_a
            act_hist.append(last_act[0])
        if pend[0] is not None:
            p_hh, p_t4 = pend[0]
            t_flush = sc.add("scalar", lambda s: s.activation(
                out=varrow[:, 0:8], in_=varrow[:, 0:8], func=AF.Square),
                waits=[t_h[-1] if t_h else None, last_act[0]])
            t_c = sc.add("vector", (lambda hh, t4: lambda v: v.tensor_copy(
                out=h_buf[:, hh, t4 * 512:(t4 + 1) * 512],
                in_=h16[t4][:, :]))(p_hh, p_t4), waits=[t_flush])
            t_h.append(t_c)
        if early_tap:
            the = dout("dbg_hbuf_early", [128, 32, HALF], FP8)
            t_he = sc.add("gpsimd", lambda g: g.dma_start(
                out=the, in_=h_buf[:, :, :]), waits=[t_h[-1]], dma=True)
            t_h.append(t_he)
        t_out = []
        prev_w2 = [None, None]
        prev_mm2 = [None, None]
        act2_hist = []
        prev_mk = [None, None, None, None]
        prev_y16 = [None, None, None, None]
        prev_md = [None]
        prev_md2 = [None, None]
        for dt2 in range(8):
            b = dt2 % 2
            bs = 4 * (dt2 % 2)
            t_w = [sc.add(("sync", "gpsimd", "scalar", "sync")[wi],
                (lambda dd, bb, t0, t1: lambda s: s.dma_start(
                    out=w2s[bb][:, t0:t1, :],
                    in_=w2q_d[:, dd * 4096 + t0 * 128:dd * 4096 + t1 * 128]
                    .rearrange("p (t n) -> p t n", n=128)))(dt2, b, 8 * wi, 8 * wi + 8),
                waits=[prev_w2[b], t_h[-1]], dma=True)
                for wi in range(4)]
            t_mmp = sc.add("tensor", (lambda dd, bb, bss: lambda t: [
                t.matmul(ps[bss + t4][:, :], w2s[bb][:, 2 * h4:2 * h4 + 2, :],
                         h_buf[:, 2 * h4:2 * h4 + 2, t4 * 512:(t4 + 1) * 512],
                         start=(h4 == 0), stop=(h4 == 15),
                         skip_group_check=True, perf_mode=PM.DoubleRow)
                for h4 in range(16) for t4 in range(4)][-1])(dt2, b, bs),
                waits=t_w + [t_h[-1], prev_mm2[b],
                             act2_hist[dt2 - 2] if dt2 >= 2 else None])
            prev_w2[b] = t_mmp
            prev_mm2[b] = t_mmp
            t_last_dt = None
            t_a = None
            for tc in range(4):
                mb = tc
                t_mk = sc.add(("sync", "gpsimd", "scalar")[(dt2 * 4 + tc) % 3],
                    (lambda dd, t4, bb: lambda s: s.dma_start(
                        out=mk16[bb][:, :],
                        in_=dmask16[dd * 128:(dd + 1) * 128, t4 * 512:(t4 + 1) * 512]))(
                        dt2, tc, tc), waits=[prev_mk[tc], t_xq_toks[0]], dma=True)
                t_a = sc.add("scalar", (lambda dd, bss, t4: lambda s: s.activation(
                    out=h16[t4][:, :], in_=ps[bss + t4][:, :], func=AF.Identity,
                    bias=b2q[:, dd:dd + 1]))(dt2, bs, tc),
                    waits=[t_mmp, prev_y16[tc]])
                t_mda = sc.add("gpsimd", (lambda t4, bb, sb_: lambda g:
                    g.tensor_tensor(out=md2_s[sb_][:, :], in0=h16[t4][:, :],
                                    in1=mk16[bb][:, :], op=ALU.mult))(tc, mb, tc % 2),
                    waits=[t_a, t_mk, prev_md2[tc % 2]])
                t_md = sc.add("vector", (lambda dd, t4, sb_: lambda v:
                    v.tensor_tensor(out=resid[:, dd, t4 * 512:(t4 + 1) * 512],
                                    in0=resid[:, dd, t4 * 512:(t4 + 1) * 512],
                                    in1=md2_s[sb_][:, :], op=ALU.add))(dt2, tc, tc % 2),
                    waits=[t_mda, prev_md[0]])
                prev_md[0] = t_md
                prev_md2[tc % 2] = t_md
                prev_y16[tc] = t_md
                prev_mk[tc] = t_md
                t_last_dt = t_md
            act2_hist.append(t_a)
            t_out.append(t_last_dt)
        return t_out

    ffn_no = [0]

    def ffn_phase(t_ready):
        ffn_no[0] += 1
        fn_ = ffn_no[0]
        sp = ExitStack()
        ef = sp.enter_context
        xq = ef(nc.sbuf_tensor(f"xq_{fn_}", [128, 8, HALF], FP8))
        h_buf = ef(nc.sbuf_tensor(f"h_buf_{fn_}", [128, 32, HALF], FP8))
        w1s = [ef(nc.sbuf_tensor(f"w1s_{fn_}_{i}", [128, 8, 128], FP8)) for i in range(2)]
        w2s = [ef(nc.sbuf_tensor(f"w2s_{fn_}_{i}", [128, 32, 128], FP8)) for i in range(2)]
        h16 = [ef(nc.sbuf_tensor(f"h16_{fn_}_{i}", [128, 512], F16)) for i in range(4)]
        mk16 = [ef(nc.sbuf_tensor(f"mk16_{fn_}_{i}", [128, 512], FP8)) for i in range(4)]
        t_xq = emit_ln_quant(t_ready, xq)
        if "ffn0" in dbg and fn_ == 1:
            txq = dout("dbg_xq", [128, 8, HALF], FP8)
            t_tap = sc.add("gpsimd", lambda g: g.dma_start(
                out=txq, in_=xq[:, :, :]), waits=t_xq, dma=True)
            t_xq = t_xq + [t_tap]
        t_o = emit_ffn(t_xq, (xq, h_buf, w1s, w2s, h16, mk16),
                       early_tap=("ffn0" in dbg and fn_ == 1))
        if "ffn0" in dbg and fn_ == 1:
            th = dout("dbg_hbuf", [128, 32, HALF], FP8)
            t_tap2 = sc.add("gpsimd", lambda g: g.dma_start(
                out=th, in_=h_buf[:, :, :]), waits=t_o, dma=True)
            t_o = t_o + [t_tap2]
        sp.close()
        return t_o

    if "w1q" in dbg:
        tw1 = dout("dbg_w1q", [128, HID], FP8)
        tw2 = dout("dbg_w2q", [128, 2048], FP8)
        w1chk = eb_(nc.sbuf_tensor("w1chk", [128, HID], FP8))
        w2chk = eb_(nc.sbuf_tensor("w2chk", [128, 2048], FP8))
        t_wl = sc.add("sync", lambda s: s.dma_start(
            out=w1chk[:, :], in_=w1q_d[0:128, :]), waits=junction, dma=True)
        t_wl2 = sc.add("sync", lambda s: s.dma_start(
            out=w2chk[:, 0:1024], in_=w2q_d[0:128, :]), waits=junction, dma=True)
        t_wl3 = sc.add("sync", lambda s: s.dma_start(
            out=w2chk[:, 1024:2048], in_=w2q_d[3968:4096, :]), waits=junction, dma=True)
        sc.add("gpsimd", lambda g: g.dma_start(out=tw1, in_=w1chk[:, :]),
               waits=[t_wl], dma=True)
        sc.add("gpsimd", lambda g: g.dma_start(out=tw2, in_=w2chk[:, :]),
               waits=[t_wl2, t_wl3], dma=True)
    t_ffn1 = ffn_phase(t_ffn_in)

    if "resid2" in dbg:
        tr2 = dout("dbg_resid2", [128, 8, HALF], F32)
        t_ffn1 = t_ffn1 + [sc.add("gpsimd", lambda g: g.dma_start(
            out=tr2, in_=resid[:, :, :]), waits=t_ffn1[:], dma=True)]

    if stop_after == "ffn1":
        t_fin = sc.add("gpsimd", lambda g: g.dma_start(
            out=y_out.rearrange("(t p) n -> p t n", p=128), in_=resid[:, :, :]),
            waits=t_ffn1, dma=True)
        sc.add("gpsimd", lambda g: g.engine_nop(), waits=[t_fin])
        scopeB.close()
        block = root.enter_context(nc.Block())
        sc.emit(nc, block, root)
        apply_ldw_skip()
        root.close()
        return nc

    scopeS = ExitStack()
    es = scopeS.enter_context
    u_bf = es(nc.sbuf_tensor("u_bf", [128, 8, HALF], BF16))
    bu = es(nc.sbuf_tensor("bu", [128, 2, HALF], F32))
    lamb = es(nc.sbuf_tensor("lamb", [128, 2, HALF], F32))
    hsc = es(nc.sbuf_tensor("hsc", [128, 2, HALF], F32))
    h_bf = es(nc.sbuf_tensor("h_bf", [128, 2, HALF], BF16))
    onesw = es(nc.sbuf_tensor("onesw", [128, HALF], F32))
    carry_sb = es(nc.sbuf_tensor("carry_sb", [128, 2], F32))
    cin_sb = es(nc.sbuf_tensor("cin_sb", [128, 2], F32))

    t_ln3 = []
    t_prev = None
    for ch in range(4):
        c0 = ch * 512

        def dst3(v, d, bmu, br, _c0=c0):
            return [
                lambda v2: v2.tensor_tensor(out=xc_s[:, :],
                                            in0=resid[:, d, _c0:_c0 + 512],
                                            in1=ps[bmu][:, :], op=ALU.subtract),
                lambda v2: v2.tensor_tensor(out=xf_s[:, :], in0=xc_s[:, :],
                                            in1=ps[br][:, :], op=ALU.mult),
                lambda v2: last(
                    v2.tensor_copy(out=u_bf[:, d, _c0:_c0 + 512], in_=xf_s[:, :]),
                    v2.tensor_scalar(out=md_s[:, :], in0=xf_s[:, :],
                                     scalar1=d_sb[:, d:d + 1], scalar2=None,
                                     op0=ALU.mult)),
                lambda v2: v2.tensor_tensor(out=resid[:, d, _c0:_c0 + 512],
                                            in0=resid[:, d, _c0:_c0 + 512],
                                            in1=md_s[:, :], op=ALU.add),
            ]
        t_norm = emit_ln(512, (lambda _c0=c0: lambda d: resid[:, d, _c0:_c0 + 512])(),
                         dst3, t_ffn1, ch % 2)
        t_ln3.extend(t_norm)

    t_bu = []
    last_bank = {}
    for nt in range(2):
        for tc in range(4):
            bank = (nt * 4 + tc) % 4
            t_mmp = sc.add("tensor", (lambda nn, t4, bk: lambda t: [
                t.matmul(ps[bk][:, :], bwt_bf[:, d, nn * 128:(nn + 1) * 128],
                         u_bf[:, d, t4 * 512:(t4 + 1) * 512],
                         start=(d == 0), stop=(d == 7))
                for d in range(8)][-1])(nt, tc, bank),
                waits=t_ln3 + [last_bank.get(bank)])
            t_ev = sc.add("vector", (lambda nn, t4, bk: lambda v: v.tensor_copy(
                out=bu[:, nn, t4 * 512:(t4 + 1) * 512], in_=ps[bk][:, :]))(
                    nt, tc, bank), waits=[t_mmp])
            last_bank[bank] = t_ev
            t_bu.append(t_ev)

    t_lb0 = sc.add("vector", lambda v: v.memset(onesw[:, :], 1.0),
                   waits=[t_lam] + t_ffn1)
    t_lb = sc.add("vector", lambda v: last(
        v.tensor_scalar(out=lamb[:, 0, :], in0=onesw[:, :],
                        scalar1=lam_col[:, 0:1], scalar2=None, op0=ALU.mult),
        v.tensor_scalar(out=lamb[:, 1, :], in0=onesw[:, :],
                        scalar1=lam_col[:, 1:2], scalar2=None, op0=ALU.mult),
    ), hard=[t_lb0])

    t_s1a = sc.add("vector", lambda v: last(
        v.tensor_tensor_scan(out=hsc[:, 0, :], data0=lamb[:, 0, :],
                             data1=bu[:, 0, :], initial=0.0,
                             op0=ALU.mult, op1=ALU.add),
        v.tensor_tensor_scan(out=hsc[:, 1, :], data0=lamb[:, 1, :],
                             data1=bu[:, 1, :], initial=0.0,
                             op0=ALU.mult, op1=ALU.add),
    ), waits=[t_lb] + t_bu)
    t_s1 = sc.add("vector", lambda v: last(
        v.tensor_scalar(out=carry_sb[:, 0:1], in0=hsc[:, 0, HALF - 1:HALF],
                        scalar1=gA_sb[:, 0:1], scalar2=None, op0=ALU.mult),
        v.tensor_scalar(out=carry_sb[:, 1:2], in0=hsc[:, 1, HALF - 1:HALF],
                        scalar1=gA_sb[:, 0:1], scalar2=None, op0=ALU.mult),
    ), hard=[t_s1a])
    t_cst = sc.add("gpsimd", lambda g: g.dma_start(
        out=carry_in_d.ap(), in_=carry_sb[:, :]), waits=[t_s1], dma=True)
    t_cc2 = sc.add("gpsimd", lambda g: g.collective_compute(
        "AllGather", ALU.bypass,
        replica_groups=[[0, 1], [2, 3], [4, 5], [6, 7]],
        ins=[carry_in_d[:, :]], outs=[carry_out_d[:, :]]),
        waits=[t_cst], own_sem=True)
    t_cld = sc.add("gpsimd", lambda g: g.dma_start(
        out=cin_sb[:, :], in_=carry_out_d.ap()[0:128, :]),
        waits=[t_cc2], dma=True)

    t_cing = sc.add("vector", lambda v: v.tensor_scalar(
        out=cin_sb[:, :], in0=cin_sb[:, :],
        scalar1=gB_sb[:, 0:1], scalar2=None, op0=ALU.mult),
        waits=[t_cld, t_s1])
    t_s2a = sc.add("vector", lambda v: last(
        v.tensor_tensor_scan(out=hsc[:, 0, :], data0=lamb[:, 0, :],
                             data1=bu[:, 0, :], initial=cin_sb[:, 0:1],
                             op0=ALU.mult, op1=ALU.add),
        v.tensor_tensor_scan(out=hsc[:, 1, :], data0=lamb[:, 1, :],
                             data1=bu[:, 1, :], initial=cin_sb[:, 1:2],
                             op0=ALU.mult, op1=ALU.add),
    ), hard=[t_cing])
    t_s2 = sc.add("vector", lambda v: last(
        v.tensor_copy(out=h_bf[:, 0, :], in_=hsc[:, 0, :]),
        v.tensor_copy(out=h_bf[:, 1, :], in_=hsc[:, 1, :]),
    ), hard=[t_s2a])

    t_cp = []
    last_bank = {}
    for dt2 in range(8):
        t_last_dt = None
        for tc in range(4):
            bank = (dt2 * 4 + tc) % 4
            t_mmp = sc.add("tensor", (lambda dd, t4, bk: lambda t: [
                t.matmul(ps[bk][:, :], cwt_bf[:, nn, dd * 128:(dd + 1) * 128],
                         h_bf[:, nn, t4 * 512:(t4 + 1) * 512],
                         start=(nn == 0), stop=(nn == 1))
                for nn in range(2)][-1])(dt2, tc, bank),
                waits=[t_s2, last_bank.get(bank)])
            t_ev = sc.add("vector", (lambda dd, t4, bk: lambda v: v.tensor_tensor(
                out=resid[:, dd, t4 * 512:(t4 + 1) * 512],
                in0=resid[:, dd, t4 * 512:(t4 + 1) * 512],
                in1=ps[bk][:, :], op=ALU.add))(dt2, tc, bank),
                waits=[t_mmp])
            last_bank[bank] = t_ev
            t_last_dt = t_ev
        t_cp.append(t_last_dt)

    scopeS.close()

    if "resid3" in dbg:
        tr3 = dout("dbg_resid3", [128, 8, HALF], F32)
        t_cp = t_cp + [sc.add("gpsimd", lambda g: g.dma_start(
            out=tr3, in_=resid[:, :, :]), waits=t_cp[:], dma=True)]

    t_ffn2 = ffn_phase(t_cp)

    t_yo = sc.add("gpsimd", lambda g: g.dma_start(
        out=y_out.rearrange("(t p) n -> p t n", p=128), in_=resid[:, :, :]),
        waits=t_ffn2, dma=True)
    sc.add("gpsimd", lambda g: g.engine_nop(), waits=[t_yo])

    scopeB.close()
    block = root.enter_context(nc.Block())
    sc.emit(nc, block, root)
    apply_ldw_skip()
    root.close()
    return nc



def _np_keep_mask():
    def threefry2x32(k1, k2, x1, x2):
        def rotl(x, d):
            return ((x << np.uint32(d)) | (x >> np.uint32(32 - d))).astype(np.uint32)
        ks0, ks1 = np.uint32(k1), np.uint32(k2)
        ks2 = np.uint32(ks0 ^ ks1 ^ np.uint32(0x1BD11BDA))
        x1 = (x1 + ks0).astype(np.uint32)
        x2 = (x2 + ks1).astype(np.uint32)
        rotations = [(13, 15, 26, 6), (17, 29, 16, 24)]
        ks = [ks0, ks1, ks2]
        for i in range(5):
            for rot in rotations[i % 2]:
                x1 = (x1 + x2).astype(np.uint32)
                x2 = rotl(x2, rot)
                x2 = (x2 ^ x1).astype(np.uint32)
            x1 = (x1 + ks[(i + 1) % 3]).astype(np.uint32)
            x2 = (x2 + ks[(i + 2) % 3] + np.uint32(i + 1)).astype(np.uint32)
        return x1, x2

    shape = (4, SEQ, D_MODEL)
    n = int(np.prod(shape))
    idx = np.arange(n, dtype=np.uint64)
    hi = (idx >> np.uint64(32)).astype(np.uint32)
    lo = (idx & np.uint64(0xFFFFFFFF)).astype(np.uint32)
    with np.errstate(over="ignore"):
        b1, b2 = threefry2x32(0, 42, hi, lo)
    bits = b1 ^ b2
    f = ((bits >> np.uint32(9)) | np.uint32(0x3F800000)).view(np.float32) - np.float32(1.0)
    return (f < np.float32(0.8)).reshape(shape)


def make_inputs(kw):
    x = np.asarray(kw["L0_attn_a_x"], np.float32)
    Q = np.asarray(kw["L0_attn_a_Q"], np.float32)
    K = np.asarray(kw["L0_attn_a_K"], np.float32)
    V = np.asarray(kw["L0_attn_a_V"], np.float32)
    O = np.asarray(kw["L0_attn_a_O"], np.float32)
    ll = np.asarray(kw["L0_attn_b_log_lambda"], np.float32)
    Bw = np.asarray(kw["L0_attn_b_B_w"], np.float32)
    Cw = np.asarray(kw["L0_attn_b_C_w"], np.float32)
    D = np.asarray(kw["L0_attn_b_D"], np.float32)
    w1 = np.asarray(kw["L0_ffn_w1"], np.float32)
    b1 = np.asarray(kw["L0_ffn_b1"], np.float32)
    w2 = np.asarray(kw["L0_ffn_w2"], np.float32)
    b2 = np.asarray(kw["L0_ffn_b2"], np.float32)

    keep = _np_keep_mask()
    mask16 = (keep.astype(np.float32) * np.float32(1.25)).astype(ml_dtypes.float8_e4m3)

    tril = np.tril(np.ones((128, 128), np.float32))
    masknegqk = ((1.0 - tril) * np.float32(NEG_BIG)).astype(np.float32)
    keepkq = np.triu(np.ones((128, 128), np.float32)).astype(ml_dtypes.bfloat16)
    mnegkq = ((1.0 - np.triu(np.ones((128, 128), np.float32))) *
              np.float32(NEG_BIG)).astype(np.float32)
    mneg4 = np.hstack([
        mnegkq, np.zeros((128, 128), np.float32),
        np.full((128, 128), NEG_BIG, np.float32), mnegkq,
    ]).astype(ml_dtypes.bfloat16)
    ident = np.eye(128, dtype=np.float32)

    w1t = np.ascontiguousarray(
        w1.T.reshape(8, 128, 32, 128).transpose(1, 2, 0, 3).reshape(128, 32768))
    w2t = np.ascontiguousarray(
        w2.T.reshape(32, 128, 8, 128).transpose(1, 2, 0, 3).reshape(128, 32768))
    bwt = np.ascontiguousarray(Bw.T)
    cwt = np.ascontiguousarray(Cw.T)
    lamlog = np.ascontiguousarray(ll.reshape(2, 128).T)

    in_maps = []
    for c in range(8):
        b, half = c // 2, c % 2
        wqk = np.empty((1024, 1024), np.float32)
        wvl = np.empty((1024, 512), np.float32)
        for j in range(8):
            h = 8 * half + j
            wqk[:, j * 128:j * 128 + 64] = Q[:, h, :]
            wqk[:, j * 128 + 64:j * 128 + 128] = K[:, h, :]
            wvl[:, j * 64:(j + 1) * 64] = V[:, h, :]
        sl = slice(HALF * half, HALF * half + HALF)
        im = dict(
            x_fm=np.ascontiguousarray(x[b].T),
            xmine_fm=np.ascontiguousarray(
                x[b, sl].T.reshape(8, 128, HALF).transpose(1, 0, 2)
                .reshape(128, 8 * HALF)),
            wqk=wqk, wv=wvl, wo=O,
            w1t=w1t, w2t=w2t,
            b1c=np.ascontiguousarray(b1.reshape(32, 128).T),
            b2c=np.ascontiguousarray(b2.reshape(8, 128).T),
            bwt=bwt, cwt=cwt, lamlog=lamlog,
            dcol=np.ascontiguousarray(D.reshape(8, 128).T),
            dmask16=np.ascontiguousarray(mask16[b, sl].T),
            masknegqk=masknegqk, keepkq=keepkq, mnegkq=mnegkq, ident=ident,
            mneg4=mneg4,
            gA=np.full((128, 1), 1.0 - half, np.float32),
            gB=np.full((128, 1), float(half), np.float32),
        )
        in_maps.append(im)
    return in_maps


def kernel(**kw):
    in_maps = make_inputs(kw)
    nc = build_program()
    res = run_bass_kernel_spmd(nc, in_maps, core_ids=list(range(8)))
    out = np.empty((4, SEQ, D_MODEL), np.float32)
    for c in range(8):
        b, half = c // 2, c % 2
        out[b, HALF * half:HALF * half + HALF, :] = res.results[c]["y_out"].T
    return out


# version 17
# speedup vs baseline: 1.0361x; 1.0361x over previous
import numpy as np
import ml_dtypes
from contextlib import ExitStack

import concourse.bass as bass
import concourse.mybir as mybir
from concourse.bass_utils import run_bass_kernel_spmd

F32 = mybir.dt.float32
BF16 = mybir.dt.bfloat16
F16 = mybir.dt.float16
FP8 = mybir.dt.float8e4
AF = mybir.ActivationFunctionType
PM = mybir.MatmulPerfMode
ALU = mybir.AluOpType
AXX = mybir.AxisListType.X

D_MODEL = 1024
SEQ = 4096
HALF = 2048
NH_LOC = 8
HID = 4096
NSTATE = 256
NEG_BIG = -30000.0

ENGS = ("sync", "tensor", "vector", "scalar", "gpsimd")


class Tok:
    __slots__ = ("eng", "rec")

    def __init__(self, eng, rec):
        self.eng = eng
        self.rec = rec


NDMASLOT = 4


class Sched:

    def __init__(self):
        self.ops = {e: [] for e in ENGS}
        self.dma_idx = {e: 0 for e in ENGS}
        self.dma_prev = {e: [None] * NDMASLOT for e in ENGS}

    def add(self, eng, fn, waits=(), dma=False, own_sem=False, hard=()):
        rec = {"fn": fn, "waits": [], "inc": 16 if dma else 1,
               "need": bool(dma), "own": own_sem, "val": None, "sem": None,
               "slot": None}
        if dma and not own_sem:
            slot = self.dma_idx[eng] % NDMASLOT
            self.dma_idx[eng] += 1
            rec["slot"] = slot
            prev = self.dma_prev[eng][slot]
            if prev is not None:
                prev.rec["need"] = True
                rec["waits"].append(prev)
        for w in list(waits) + list(hard):
            if w is None:
                continue
            w.rec["need"] = True
            rec["waits"].append(w)
        tok = Tok(eng, rec)
        if dma and not own_sem:
            self.dma_prev[eng][rec["slot"]] = tok
        self.ops[eng].append(rec)
        return tok

    def emit(self, nc, block, stack):
        sems = {}
        dsems = {}
        own_i = [0]
        for e in ENGS:
            sems[e] = stack.enter_context(nc.semaphore(f"sem_{e}"))
            if self.dma_idx[e]:
                dsems[e] = [stack.enter_context(nc.semaphore(f"dsem_{e}{i}"))
                            for i in range(NDMASLOT)]
            cnt = 0
            dcnt = [0] * NDMASLOT
            for rec in self.ops[e]:
                if rec["own"]:
                    own_i[0] += 1
                    rec["sem"] = stack.enter_context(
                        nc.semaphore(f"sem_own{own_i[0]}"))
                    rec["val"] = rec["inc"]
                elif rec["slot"] is not None:
                    s = rec["slot"]
                    dcnt[s] += 16
                    rec["sem"] = dsems[e][s]
                    rec["val"] = dcnt[s]
                elif rec["need"]:
                    cnt += rec["inc"]
                    rec["sem"] = sems[e]
                    rec["val"] = cnt

        def section(engine, e):
            waited = {}
            for rec in self.ops[e]:
                for w in rec["waits"]:
                    wr = w.rec
                    key = id(wr["sem"])
                    if waited.get(key, -1) >= wr["val"]:
                        continue
                    engine.wait_ge(wr["sem"], wr["val"])
                    waited[key] = wr["val"]
                inst = rec["fn"](engine)
                if rec["need"] or rec["own"]:
                    inst.then_inc(rec["sem"], rec["inc"])

        @block.sync
        def _(sync):
            section(sync, "sync")

        @block.tensor
        def _(tensor):
            section(tensor, "tensor")

        @block.vector
        def _(vector):
            section(vector, "vector")

        @block.scalar
        def _(scalar):
            section(scalar, "scalar")

        @block.gpsimd
        def _(gpsimd):
            section(gpsimd, "gpsimd")


def last(*insts):
    return insts[-1]


def build_program(stop_after="full", debug=(), skip_attn=False):
    nc = bass.Bass("TRN2", target_bir_lowering=False)
    sc = Sched()
    dbg = set(debug)
    ldw_skip = []

    def mark_ldw(inst):
        ldw_skip.append(inst.ins.name)
        return inst

    def apply_ldw_skip():
        import dataclasses
        import os as _os
        if _os.environ.get("BASS_SIM_FP64") == "1" or \
                _os.environ.get("KERN_NO_LDW") == "1":
            return
        names = set(ldw_skip)
        if not names:
            return
        for bb in nc.m.functions[0].blocks:
            insts = list(bb.instructions)
            ch = False
            for i, ins in enumerate(insts):
                if getattr(ins, "name", None) in names:
                    insts[i] = dataclasses.replace(ins, ldweights=False)
                    ch = True
            if ch:
                bb.instructions = insts

    def din(name, shape, dt=F32):
        return nc.dram_tensor(name, shape, dt, kind="ExternalInput").ap()

    x_fm = din("x_fm", [D_MODEL, SEQ])
    xmine_fm = din("xmine_fm", [128, 8 * HALF])
    wqk_in = din("wqk", [D_MODEL, 1024])
    wv_in = din("wv", [D_MODEL, 512])
    wo_in = din("wo", [D_MODEL, D_MODEL])
    w1t_in = din("w1t", [128, 32768])
    w2t_in = din("w2t", [128, 32768])
    b1c = din("b1c", [128, 32])
    b2c = din("b2c", [128, 8])
    bwt_in = din("bwt", [D_MODEL, NSTATE])
    cwt_in = din("cwt", [NSTATE, D_MODEL])
    lamlog = din("lamlog", [128, 2])
    dcol = din("dcol", [128, 8])
    dmask16 = din("dmask16", [D_MODEL, HALF], FP8)
    masknegqk = din("masknegqk", [128, 128])
    keepkq = din("keepkq", [128, 128], BF16)
    mnegkq = din("mnegkq", [128, 128])
    mneg4 = din("mneg4", [128, 512], BF16)
    gA = din("gA", [128, 1])
    gB = din("gB", [128, 1])
    ident_in = din("ident", [128, 128])

    y_out = nc.dram_tensor("y_out", [D_MODEL, HALF], F32, kind="ExternalOutput").ap()

    dbg_outs = {}

    def dout(name, shape, dt):
        dbg_outs[name] = nc.dram_tensor(name, shape, dt, kind="ExternalOutput").ap()
        return dbg_outs[name]

    w1q_d = nc.dram_tensor("w1q_d", [128, 32768], FP8).ap()
    w2q_d = nc.dram_tensor("w2q_d", [128, 32768], FP8).ap()
    mrow_d = nc.dram_tensor("mrow_d", [NH_LOC * 32, 128], BF16).ap()
    og_in = nc.dram_tensor("og_in", [512, SEQ], BF16)
    og_out_c = [nc.dram_tensor(f"og_out_c{i}", [256, SEQ], BF16)
                for i in range(4)]
    carry_in_d = nc.dram_tensor("carry_in_d", [128, 2], F32)
    carry_out_d = nc.dram_tensor("carry_out_d", [256, 2], F32)

    root = ExitStack()
    ep = root.enter_context

    ps = [ep(nc.psum_tensor(f"ps{i}", [128, 512], F32)) for i in range(8)]

    bwt_bf = ep(nc.sbuf_tensor("bwt_bf", [128, 8, 256], BF16))
    cwt_bf = ep(nc.sbuf_tensor("cwt_bf", [128, 2, 1024], BF16))
    b1q = ep(nc.sbuf_tensor("b1q", [128, 32], F32))
    b2q = ep(nc.sbuf_tensor("b2q", [128, 8], F32))
    lam_col = ep(nc.sbuf_tensor("lam_col", [128, 2], F32))
    d_sb = ep(nc.sbuf_tensor("d_sb", [128, 8], F32))
    mneg_sb = ep(nc.sbuf_tensor("mneg_sb", [128, 128], F32))
    mneg4_sb = ep(nc.sbuf_tensor("mneg4_sb", [128, 512], BF16))
    id_sb = ep(nc.sbuf_tensor("id_sb", [128, 128], F32))
    gA_sb = ep(nc.sbuf_tensor("gA_sb", [128, 1], F32))
    gB_sb = ep(nc.sbuf_tensor("gB_sb", [128, 1], F32))
    onesrow_f = ep(nc.sbuf_tensor("onesrow_f", [1, 128], F32))
    ones_col = ep(nc.sbuf_tensor("ones_col", [128, 1], F32))
    murow = ep(nc.sbuf_tensor("murow", [1, 512], F32))
    m2row = ep(nc.sbuf_tensor("m2row", [1, 512], F32))
    varrow = ep(nc.sbuf_tensor("varrow", [1, 512], F32))
    rrow = ep(nc.sbuf_tensor("rrow", [1, 512], F32))
    b1_8 = ep(nc.sbuf_tensor("b1_8", [128, 32], FP8))
    b2_8 = ep(nc.sbuf_tensor("b2_8", [128, 8], FP8))
    sq = [ep(nc.sbuf_tensor(f"sq{i}", [128, 512], F32)) for i in range(4)]

    _small_loads = [
        (b1q, b1c), (b2q, b2c), (lam_col, lamlog), (d_sb, dcol),
        (mneg_sb, masknegqk), (id_sb, ident_in),
        (mneg4_sb, mneg4),
        (gA_sb, gA), (gB_sb, gB),
    ]
    for _dst, _src in _small_loads:
        t_small = sc.add("sync", (lambda d_, s_: lambda s: s.dma_start(
            out=d_[:, :], in_=s_))(_dst, _src), dma=True)

    t_c1 = sc.add("vector", lambda v: last(
        v.memset(onesrow_f[:, :], 1.0),
        v.memset(ones_col[:, :], 1.0),
        v.tensor_copy(out=b1_8[:, :], in_=b1q[:, :]),
        v.tensor_copy(out=b2_8[:, :], in_=b2q[:, :]),
    ), waits=[t_small])
    t_consts = sc.add("vector", lambda v: last(
        v.tensor_copy(out=b1q[:, :], in_=b1_8[:, :]),
        v.tensor_copy(out=b2q[:, :], in_=b2_8[:, :]),
    ), hard=[t_c1])

    t_lam = sc.add("scalar", lambda s: s.activation(
        out=lam_col[:, :], in_=lam_col[:, :], func=AF.Sigmoid), waits=[t_small])

    scopeA = ExitStack()
    ea = scopeA.enter_context
    q_all = ea(nc.sbuf_tensor("q_all", [128, 4, SEQ], BF16))
    k_pack = ea(nc.sbuf_tensor("k_pack", [128, 4, SEQ], BF16))
    v_buf = ea(nc.sbuf_tensor("v_buf", [128, 32, 520], BF16))

    scopeA0 = ExitStack()
    e0 = scopeA0.enter_context
    wqk_bf = e0(nc.sbuf_tensor("wqk_bf", [128, 8, 1024], BF16))
    wv_bf = e0(nc.sbuf_tensor("wv_bf", [128, 8, 512], BF16))
    wstream = [e0(nc.sbuf_tensor(f"wstream{i}", [128, 2048], F32)) for i in range(2)]
    w8buf = [e0(nc.sbuf_tensor(f"w8buf{i}", [128, 2048], FP8)) for i in range(2)]

    last_cast = [None, None]
    last_store = [None, None]

    def stream_weight(src_full, n_tiles, cols, dst_sb=None, dst_dma=None,
                      col_tiles=False):
        toks = []
        for i in range(n_tiles):
            b = i % 2
            if col_tiles:
                src = src_full[:, i * cols:(i + 1) * cols]
            else:
                src = src_full[i * 128:(i + 1) * 128, :]
            t_load = sc.add(
                ("sync", "gpsimd")[i % 2],
                (lambda bb, cc, sp: lambda s: s.dma_start(
                    out=wstream[bb][:, 0:cc], in_=sp))(b, cols, src),
                waits=[last_cast[b]], dma=True)
            if dst_dma is None:
                t_c = sc.add(
                    "vector",
                    (lambda bb, cc, ii: lambda v: v.tensor_copy(
                        out=dst_sb(ii), in_=wstream[bb][:, 0:cc]))(b, cols, i),
                    waits=[t_load])
                last_cast[b] = t_c
                toks.append(t_c)
            else:
                t_c = sc.add(
                    "vector",
                    (lambda bb, cc: lambda v: v.tensor_copy(
                        out=w8buf[bb][:, 0:cc], in_=wstream[bb][:, 0:cc]))(b, cols),
                    waits=[t_load, last_store[b]])
                last_cast[b] = t_c
                t_s = sc.add(
                    ("gpsimd", "sync")[i % 2],
                    (lambda bb, cc, dp: lambda g: g.dma_start(
                        out=dp, in_=w8buf[bb][:, 0:cc]))(
                            b, cols, dst_dma[:, i * cols:(i + 1) * cols]),
                    waits=[t_c], dma=True)
                last_store[b] = t_s
                toks.append(t_s)
        return toks

    tw_qk = stream_weight(wqk_in, 8, 1024, dst_sb=lambda i: wqk_bf[:, i, :])
    tw_v = stream_weight(wv_in, 8, 512, dst_sb=lambda i: wv_bf[:, i, 0:512])
    tw_bw = stream_weight(bwt_in, 8, 256, dst_sb=lambda i: bwt_bf[:, i, 0:256])
    tw_cw = stream_weight(cwt_in, 2, 1024, dst_sb=lambda i: cwt_bf[:, i, :])
    tw_1 = stream_weight(w1t_in, 16, 2048, dst_dma=w1q_d, col_tiles=True)
    tw_2 = stream_weight(w2t_in, 16, 2048, dst_dma=w2q_d, col_tiles=True)
    t_wdone = ([tw_qk[-1], tw_v[-1], tw_bw[-1], tw_cw[-1], t_consts]
               + tw_1[-4:] + tw_2[-4:])

    CH = 256
    xs = [e0(nc.sbuf_tensor(f"xs{i}", [128, 8, CH], F32)) for i in range(2)]
    xln = e0(nc.sbuf_tensor("xln", [128, 8, 2 * CH], BF16))

    ln_mm_prev = [None]
    ln_bc_prev = [None]
    ln_sd_prev = [None]
    ln_norm_done = [None, None]
    ln_norm_chain = [None]
    qkv_war = [None]

    def emit_ln(width, src_ap_fn, dst_write, waits_src, pha):
        bmu, br = (5, 6) if pha == 0 else (4, 7)
        t_mu = None
        for dt in range(8):
            t_mu = sc.add("tensor", (lambda d, first: lambda t: t.matmul(
                ps[bmu][0:1, 0:width], ones_col[:, :], src_ap_fn(d),
                start=first, stop=(d == 7)))(dt, dt == 0),
                waits=[t_mu, ln_norm_done[pha] if dt == 0 else None]
                + list(waits_src))
        t_mm = None
        sqmm_toks = []
        for dt in range(8):
            t_sq_d = sc.add("scalar", (lambda d, ss: lambda s: s.activation(
                out=sq[ss][:, 0:width], in_=src_ap_fn(d), func=AF.Square))(
                    dt, dt % 4),
                waits=list(waits_src)
                + [ln_mm_prev[0] if dt < 4 else sqmm_toks[dt - 4]])
            t_mm = sc.add("tensor", (lambda d, ss, first: lambda t: t.matmul(
                ps[br][0:1, 0:width], ones_col[:, :], sq[ss][:, 0:width],
                start=first, stop=(d == 7)))(dt, dt % 4, dt == 0),
                waits=[t_sq_d, t_mm,
                       ln_norm_done[pha] if dt == 0 else None])
            sqmm_toks.append(t_mm)
        ln_mm_prev[0] = t_mm
        t_st1 = sc.add("vector", (lambda w: lambda v: last(
            v.tensor_scalar_mul(out=murow[:, 0:w], in0=ps[bmu][0:1, 0:w],
                                scalar1=1.0 / 1024.0),
            v.tensor_scalar_mul(out=m2row[:, 0:w], in0=ps[br][0:1, 0:w],
                                scalar1=1.0 / 1024.0),
        ))(width), waits=[t_mm, t_mu, ln_bc_prev[0]])
        t_st2 = sc.add("vector", (lambda w: lambda v: v.tensor_tensor(
            out=varrow[:, 0:w], in0=murow[:, 0:w],
            in1=murow[:, 0:w], op=ALU.mult))(width), hard=[t_st1],
            waits=[ln_sd_prev[0]])
        t_st3 = sc.add("vector", (lambda w: lambda v: v.tensor_tensor(
            out=varrow[:, 0:w], in0=m2row[:, 0:w],
            in1=varrow[:, 0:w], op=ALU.subtract))(width), hard=[t_st2])
        t_stat = sc.add("vector", (lambda w: lambda v: v.tensor_scalar_add(
            out=varrow[:, 0:w], in0=varrow[:, 0:w], scalar1=1e-5))(width),
            hard=[t_st3])
        t_sd = sc.add("scalar", (lambda w: lambda s: s.activation(
            out=varrow[:, 0:w], in_=varrow[:, 0:w], func=AF.Sqrt))(width),
            waits=[t_stat])
        ln_sd_prev[0] = t_sd
        t_r = sc.add("vector", (lambda w: lambda v: v.reciprocal(
            out=rrow[:, 0:w], in_=varrow[:, 0:w]))(width), waits=[t_sd])
        t_bc = sc.add("tensor", (lambda w: lambda t: last(
            t.matmul(ps[bmu][:, 0:w], onesrow_f[:, :], murow[:, 0:w],
                     start=True, stop=True),
            t.matmul(ps[br][:, 0:w], onesrow_f[:, :], rrow[:, 0:w],
                     start=True, stop=True),
        ))(width), waits=[t_r, t_stat])
        ln_bc_prev[0] = t_bc
        t_norm = []
        tprev = ln_norm_chain[0]
        for dt in range(8):
            fns = dst_write(None, dt, bmu, br)
            tn = None
            for f in fns:
                tn = sc.add("vector", f,
                            waits=[t_bc, qkv_war[0] if tn is None else None],
                            hard=[tn if tn is not None else tprev])
            tprev = tn
            t_norm.append(tn)
        ln_norm_chain[0] = t_norm[-1]
        ln_norm_done[pha] = t_norm[-1]
        return t_norm

    t_p1_parts = []
    prev_xs = [None, None]
    qkv_pend = [None]
    qkv_pend2 = [None]
    emit_qkv_ref = [None]
    NC1 = SEQ // CH
    last_bank = {}
    for c in ([] if skip_attn else range(NC1)):
        b = c % 2
        c0 = c * CH
        t_load = [sc.add(("sync", "gpsimd", "sync")[si],
            (lambda bb, a0, t0, t1: lambda s: s.dma_start(
                out=xs[bb][:, t0:t1, :],
                in_=x_fm[t0 * 128:t1 * 128, a0:a0 + CH].rearrange(
                    "(t p) n -> p t n", p=128)))(b, c0, lo, hi),
            waits=[prev_xs[b]] + (t_wdone if c == 0 else []), dma=True)
            for si, (lo, hi) in enumerate(((0, 3), (3, 6), (6, 8)))]

        def dstp1(v, d, bmu, br, _b=b, _c=c):
            return [
                lambda v2: v2.tensor_tensor(out=xs[_b][:, d, :], in0=xs[_b][:, d, :],
                                            in1=ps[bmu][:, 0:CH], op=ALU.subtract),
                lambda v2: v2.tensor_tensor(
                    out=xln[:, d, (_c % 2) * CH:(_c % 2) * CH + CH],
                    in0=xs[_b][:, d, :],
                    in1=ps[br][:, 0:CH], op=ALU.mult),
            ]
        if qkv_pend2[0] is not None:
            emit_qkv_ref[0](*qkv_pend2[0])
            qkv_pend2[0] = None
        t_norm = emit_ln(CH, (lambda _b=b: lambda d: xs[_b][:, d, :])(),
                         dstp1, t_load, c % 2)
        prev_xs[b] = t_norm[-1]

        def emit_qkv(c, t_norm2):
            c0p = (c - 1) * CH
            chunk_last = None
            for j in range(NH_LOC):
                bank = j % 4
                r0 = 64 * (j % 2)
                t_mmp = sc.add("tensor", (lambda jj, bk: lambda t: [
                    t.matmul(ps[bk][:, 0:2 * CH],
                             wqk_bf[:, d, jj * 128:(jj + 1) * 128],
                             xln[:, d, :], start=(d == 0), stop=(d == 7))
                    for d in range(8)][-1])(j, bank),
                    waits=t_norm2 + [last_bank.get(bank)])
                t_ev = sc.add("vector", (lambda jj, bk, a0, rr: lambda v: last(
                    v.tensor_copy(out=q_all[rr:rr + 64, jj // 2, a0:a0 + 2 * CH],
                                  in_=ps[bk][0:64, 0:2 * CH]),
                    v.tensor_copy(out=k_pack[rr:rr + 64, jj // 2, a0:a0 + 2 * CH],
                                  in_=ps[bk][64:128, 0:2 * CH]),
                ))(j, bank, c0p, r0), waits=[t_mmp])
                last_bank[bank] = t_ev
                chunk_last = t_ev
            for ti in range(4):
                tt_g = (c - 1) * 2 + ti
                bank = ti % 4
                t_mmp = sc.add("tensor", (lambda tl, bk: lambda t: [
                    t.matmul(ps[bk][:, :], xln[:, d, tl * 128:(tl + 1) * 128],
                             wv_bf[:, d, 0:512], start=(d == 0), stop=(d == 7))
                    for d in range(8)][-1])(ti, bank),
                    waits=t_norm2 + [last_bank.get(bank)])
                t_ev = sc.add("vector", (lambda tg, bk: lambda v: last(
                    v.tensor_copy(
                        out=v_buf[:, tg, :].rearrange("p (h d) -> p h d", h=8)[:, :, 0:64],
                        in_=ps[bk][:, :].rearrange("p (h d) -> p h d", h=8)),
                    v.memset(v_buf[:, tg, :].rearrange("p (h d) -> p h d", h=8)[:, :, 64:65],
                             1.0),
                ))(tt_g, bank), waits=[t_mmp])
                last_bank[bank] = t_ev
                chunk_last = t_ev
            qkv_war[0] = chunk_last
            t_p1_parts.append(chunk_last)

        emit_qkv_ref[0] = emit_qkv
        if c % 2 == 0:
            qkv_pend[0] = t_norm
        else:
            qkv_pend2[0] = (c, qkv_pend[0] + t_norm)
    if qkv_pend2[0] is not None:
        emit_qkv_ref[0](*qkv_pend2[0])
        qkv_pend2[0] = None

    t_p1_done = t_p1_parts[-1] if t_p1_parts else t_wdone[-1]
    scopeA0.close()

    if "qkv" in dbg:
        tq = dout("dbg_q", [128, 4, SEQ], BF16)
        tk = dout("dbg_k", [128, 4, SEQ], BF16)
        tv = dout("dbg_v", [128, 32, 520], BF16)
        sc.add("gpsimd", lambda g: g.dma_start(out=tq, in_=q_all[:, :, :]),
               waits=t_p1_parts, dma=True)
        sc.add("gpsimd", lambda g: g.dma_start(out=tk, in_=k_pack[:, :, :]),
               waits=t_p1_parts, dma=True)
        t_p1_done = sc.add("gpsimd", lambda g: g.dma_start(
            out=tv, in_=v_buf[:, :, :]), waits=t_p1_parts, dma=True)

    scopeA2 = ExitStack()
    e2 = scopeA2.enter_context
    o_cat = e2(nc.sbuf_tensor("o_cat", [128, 4, SEQ], BF16))
    ktil = [e2(nc.sbuf_tensor(f"ktil{i}", [65, SEQ], BF16)) for i in range(2)]
    qhat = [e2(nc.sbuf_tensor(f"qhat{i}", [65, 32, 128], BF16)) for i in range(2)]
    pbuf = [e2(nc.sbuf_tensor(f"pbuf{i}", [128, 1024], BF16)) for i in range(2)]
    mcol2 = [e2(nc.sbuf_tensor(f"mcol2_{i}", [128, 32], F32)) for i in range(2)]
    mtmp = e2(nc.sbuf_tensor("mtmp", [128, 1], F32))
    mtmp2 = e2(nc.sbuf_tensor("mtmp2", [128, 1], F32))
    dsc = e2(nc.sbuf_tensor("dsc", [128, 128], F32))
    mrow_sb = [e2(nc.sbuf_tensor(f"mrow_sb{i}", [32, 128], BF16)) for i in range(2)]
    zrow = e2(nc.sbuf_tensor("zrow", [1, 512], F32))
    zb = e2(nc.sbuf_tensor("zb", [64, 512], F32))

    t_ktA = sc.add("vector", lambda v: last(
        v.memset(ktil[0][64:65, :], 1.0),
        v.memset(ktil[1][64:65, :], 1.0),
    ), waits=[t_p1_done] + t_wdone)

    rot = [None, None]
    g_ctr = [0]
    kq_load = [None, None]
    kq_free = [None, None]
    mcol_free = [None, None]
    mrow_free = [None, None]
    ps4_free = [None]
    red_chain = [None]
    pool_head = {}
    mrst = {}

    def emit_red(aa, ww, has_d, bb, qtix, t_mm_tok):
        tok = red_chain[0]
        wnd = ww - 128 if has_d else ww
        segs = []
        if wnd > 0:
            segs.append(("r", ps[2 * aa][:, 0:min(512, wnd)]))
            if wnd > 512:
                segs.append(("r", ps[2 * aa + 1][:, 0:wnd - 512]))
        if has_d:
            doff = ww - 128
            bank_ap = (ps[2 * aa][:, doff:doff + 128] if doff < 512
                       else ps[2 * aa + 1][:, doff - 512:doff - 512 + 128])
            segs.append(("d", bank_ap))
        for si, (kind, ap) in enumerate(segs):
            mt = mtmp if si % 2 == 0 else mtmp2
            if kind == "r":
                t_r = sc.add("vector", (lambda a_, m_: lambda v:
                             v.reduce_max(out=m_[:, :], in_=a_,
                                          axis=AXX))(ap, mt),
                             waits=[t_mm_tok, tok])
            else:
                t_d = sc.add("vector", (lambda a_: lambda v:
                             v.tensor_tensor(out=dsc[:, :], in0=a_,
                                             in1=mneg_sb[:, :],
                                             op=ALU.add))(ap),
                             waits=[t_mm_tok, tok])
                t_r = sc.add("vector", (lambda m_: lambda v:
                             v.reduce_max(out=m_[:, :], in_=dsc[:, :],
                                          axis=AXX))(mt), hard=[t_d])
            tok = sc.add("vector", (lambda m_, b_, qi: lambda v:
                         v.tensor_tensor(
                             out=mcol2[b_][:, qi:qi + 1],
                             in0=mcol2[b_][:, qi:qi + 1],
                             in1=m_[:, :], op=ALU.max))(mt, bb, qtix),
                         hard=[t_r])
        red_chain[0] = tok
        return tok

    def gen_A(j):
        b = j % 2
        t_kb = sc.add("vector", (lambda bb, rr, hh: lambda v: last(
            v.tensor_copy(out=ktil[bb][0:64, :], in_=k_pack[rr:rr + 64, hh, :]),
            v.tensor_copy(
                out=qhat[bb][0:64, :, :],
                in_=q_all[rr:rr + 64, hh, :].rearrange("p (a b) -> p a b", b=128)),
        ))(b, 64 * (j % 2), j // 2), waits=[t_ktA, kq_free[b]])
        kq_load[b] = t_kb
        red_chain[0] = sc.add("vector", (lambda bb: lambda v: v.memset(
            mcol2[bb][:, :], -3.0e38))(b),
            waits=[mcol_free[b], red_chain[0]])
        for qt in range(32):
            ntile = qt + 1
            g0 = 0
            while g0 < ntile:
                gw = min(8, ntile - g0)
                a = g_ctr[0] % 2
                g_ctr[0] += 1
                has_diag = (g0 + gw == ntile)
                wcols = gw * 128
                t_mmA = sc.add("tensor", (lambda bb, q, k0, ww, aa: lambda t: [
                    (mark_ldw if off > 0 else (lambda z: z))(
                        t.matmul(ps[2 * aa + (off // 512)][:, off % 512:off % 512 + min(512, ww - off)],
                                 qhat[bb][0:64, q, :],
                                 ktil[bb][0:64, k0 * 128 + off:k0 * 128 + off + min(512, ww - off)],
                                 start=True, stop=True))
                    for off in range(0, ww, 512)][-1])(b, qt, g0, wcols, a),
                    waits=[t_kb, rot[a]])
                rot[a] = emit_red(a, wcols, has_diag, b, qt, t_mmA)
                g0 += gw
                yield
        pool_head[j] = red_chain[0]

    def emit_T(j):
        b = j % 2
        t_tr = sc.add("tensor", (lambda bb: lambda t: t.transpose(
            ps[4][0:32, 0:128], mcol2[bb][:, :], id_sb[:, :]))(b),
            waits=[pool_head[j], ps4_free[0]])
        mcol_free[b] = t_tr
        t_ng = sc.add("vector", (lambda bb: lambda v: v.tensor_scalar_mul(
            out=mrow_sb[bb][:, :], in0=ps[4][0:32, 0:128], scalar1=-1.0))(b),
            waits=[t_tr, mrow_free[b]])
        ps4_free[0] = t_ng
        t_st = sc.add("sync", (lambda bb, jj: lambda s: s.dma_start(
            out=mrow_d[jj * 32:(jj + 1) * 32, :], in_=mrow_sb[bb][:, :]))(b, j),
            waits=[t_ng], dma=True)
        mrst[j] = t_st
        mrow_free[b] = t_st

    pfree = [None, None]
    evac_hist = []
    prev_bcz = [None]
    prev_zb = [None]
    prev_evac = [None]
    pend = [None]
    zpend = [None]
    t_attn_last = []
    t_og_chunks = []
    R_ctr = [0]

    def flush_z():
        if zpend[0] is None:
            return
        R, jj, t_av = zpend[0]
        zpend[0] = None
        B = 6 + (R % 2)
        rloc = R % 8
        qt0 = rloc * 4
        hh = jj // 2
        rr = 64 * (jj % 2)
        t_rz = sc.add("vector", (lambda b_: lambda v: v.reciprocal(
            out=zrow[0:1, 0:512], in_=ps[b_][64:65, 0:512]))(B),
            waits=[t_av, prev_bcz[0]])
        t_bcz = sc.add("tensor", lambda t: t.matmul(
            ps[5][0:64, 0:512], onesrow_f[0:1, 0:64], zrow[0:1, 0:512],
            start=True, stop=True), waits=[t_rz, prev_zb[0]])
        prev_bcz[0] = t_bcz
        t_zb = sc.add("vector", lambda v: v.tensor_copy(
            out=zb[:, 0:512], in_=ps[5][0:64, 0:512]),
            waits=[t_bcz, prev_evac[0]])
        prev_zb[0] = t_zb
        t_evac = sc.add("vector", (lambda b_, q0, h_, r_: lambda v:
            v.tensor_tensor(out=o_cat[r_:r_ + 64, h_, q0 * 128:q0 * 128 + 512],
                            in0=ps[b_][0:64, 0:512], in1=zb[:, 0:512],
                            op=ALU.mult))(B, qt0, hh, rr), hard=[t_zb])
        prev_evac[0] = t_evac
        evac_hist.append(t_evac)
        if rloc == 7:
            t_attn_last.append(t_evac)
            if jj % 2 == 1:
                t_st8 = sc.add("gpsimd", (lambda h_: lambda g: g.dma_start(
                    out=og_in.ap()[h_ * 128:(h_ + 1) * 128, :],
                    in_=o_cat[:, h_, :]))(hh),
                    waits=[t_evac, t_attn_last[jj - 1]], dma=True)
                t_cc = sc.add("gpsimd", (lambda h_: lambda g: g.collective_compute(
                    "AllGather", ALU.bypass,
                    replica_groups=[[0, 1], [2, 3], [4, 5], [6, 7]],
                    ins=[og_in[h_ * 128:(h_ + 1) * 128, :]],
                    outs=[og_out_c[h_][:, :]]))(hh),
                    waits=[t_st8], own_sem=True)
                t_og_chunks.append(t_cc)

    def flush_pend():
        if pend[0] is None:
            flush_z()
            return
        jj, qp, k0, gw, pb, t_e, first, last = pend[0]
        pend[0] = None
        flush_z()
        R = R_ctr[0]
        B = 6 + (R % 2)
        paircol = (qp % 2) * 256
        gate = None
        if first and qp % 2 == 0:
            gate = evac_hist[R - 2] if R >= 2 else None
        t_av = sc.add("tensor", (lambda jj_, k0_, gw_, pb_, pc_, b_, f_, l_:
            lambda t: [
                t.matmul(ps[b_][0:65, pc_:pc_ + 256],
                         v_buf[:, k0_ + i2, 65 * jj_:65 * jj_ + 65],
                         pbuf[pb_][:, i2 * 256:(i2 + 1) * 256],
                         start=(f_ and i2 == 0), stop=(l_ and i2 == gw_ - 1),
                         skip_group_check=True)
                for i2 in range(gw_)][-1])(jj, k0, gw, pb, paircol, B, first, last),
            waits=[t_e, gate])
        pfree[pb] = t_av
        if last and qp % 2 == 1:
            zpend[0] = (R, jj, t_av)
            R_ctr[0] += 1

    def gen_B(j):
        jb = j % 2
        t_qrow = sc.add("sync", (lambda bb, jj: lambda s: s.dma_start(
            out=qhat[bb][64:65, :, :],
            in_=mrow_d[jj * 32:(jj + 1) * 32, :].rearrange(
                "a b -> (a b)")[None, :].rearrange("o (a b) -> o a b", b=128)))(jb, j),
            waits=[mrst[j], kq_load[jb]], dma=True)
        t_klast = None
        for qp in range(16):
            ntile = 2 * qp + 2
            k0 = 0
            while k0 < ntile:
                gw = min(4, ntile - k0)
                a = g_ctr[0] % 2
                pb = g_ctr[0] % 2
                g_ctr[0] += 1
                t_qb = sc.add("tensor", (lambda bb, q2, k0_, gw_, aa: lambda t: [
                    t.matmul(ps[2 * aa + (i2 // 2)][:, (i2 % 2) * 256:(i2 % 2) * 256 + 256],
                             ktil[bb][0:65, (k0_ + i2) * 128:(k0_ + i2 + 1) * 128],
                             qhat[bb][0:65, q2:q2 + 2, :].rearrange("p a b -> p (a b)"),
                             start=True, stop=True, skip_group_check=True)
                    for i2 in range(gw_)][-1])(jb, 2 * qp, k0, gw, a),
                    waits=[t_qrow, rot[a]])
                t_klast = t_qb
                flush_pend()
                is_last = (k0 + gw == ntile)
                t_m = None
                if is_last:
                    moff = (ntile - 2 - k0) * 256
                    t_m = sc.add("vector", (lambda aa, mo: lambda v:
                        v.tensor_tensor(
                            out=ps[2 * aa + (mo // 512)][:, 0:512],
                            in0=ps[2 * aa + (mo // 512)][:, 0:512],
                            in1=mneg4_sb[:, :], op=ALU.add))(a, moff),
                        waits=[t_qb])
                t_e = sc.add("scalar", (lambda aa, ww, pp: lambda s: last(
                    s.activation(out=pbuf[pp][:, 0:512],
                                 in_=ps[2 * aa][:, 0:512],
                                 func=AF.Exp, scale=8.0),
                    *([s.activation(out=pbuf[pp][:, 512:1024],
                                    in_=ps[2 * aa + 1][:, 0:512],
                                    func=AF.Exp, scale=8.0)] if ww > 512 else []),
                ))(a, gw * 256, pb), waits=[t_qb, t_m, pfree[pb]])
                rot[a] = t_e
                pend[0] = (j, qp, k0, gw, pb, t_e, k0 == 0, is_last)
                k0 += gw
                yield
        kq_free[jb] = t_klast

    if not skip_attn:
        ga = gen_A(0)
        for _ in ga:
            pass
        emit_T(0)
        BURST = 1
        for j in range(NH_LOC):
            gb = gen_B(j)
            ga = gen_A(j + 1) if j + 1 < NH_LOC else None
            done_b = False
            while not done_b:
                for _ in range(BURST):
                    if ga is None:
                        break
                    try:
                        next(ga)
                    except StopIteration:
                        ga = None
                        emit_T(j + 1)
                for _ in range(BURST):
                    try:
                        next(gb)
                    except StopIteration:
                        done_b = True
                        break
            while ga is not None:
                try:
                    next(ga)
                except StopIteration:
                    ga = None
                    emit_T(j + 1)
        flush_pend()
        flush_z()

    t_p2_done = t_attn_last[-1] if t_attn_last else t_ktA

    if "ocat" in dbg:
        toc = dout("dbg_ocat", [128, 4, SEQ], BF16)
        t_p2_done = sc.add("gpsimd", lambda g: g.dma_start(
            out=toc, in_=o_cat[:, :, :]), waits=t_attn_last, dma=True)

    if not skip_attn:
        t_og_st = t_og_chunks[-1]
        t_cc1 = t_og_chunks[-1]
    else:
        t_og_st = t_p2_done
        t_cc1 = t_p2_done

    scopeA2.close()
    scopeA.close()

    if stop_after == "p2":
        t_fin = sc.add("gpsimd", lambda g: g.dma_start(
            out=y_out, in_=xmine_fm), waits=[t_cc1, t_p2_done], dma=True)
        sc.add("gpsimd", lambda g: g.engine_nop(), waits=[t_fin])
        block = root.enter_context(nc.Block())
        sc.emit(nc, block, root)
        apply_ldw_skip()
        root.close()
        return nc

    junction = [t_cc1, t_p2_done, t_og_st]

    scopeB = ExitStack()
    eb_ = scopeB.enter_context
    resid = eb_(nc.sbuf_tensor("resid", [128, 8, HALF], F32))
    xc_s = eb_(nc.sbuf_tensor("xc_s", [128, 512], F32))
    xf_s = eb_(nc.sbuf_tensor("xf_s", [128, 512], F32))
    md_s = eb_(nc.sbuf_tensor("md_s", [128, 512], F32))
    md2_s = [eb_(nc.sbuf_tensor(f"md2_s{i}", [128, 512], F32)) for i in range(2)]

    if "yb0" in dbg:
        y_cap = eb_(nc.sbuf_tensor("y_cap", [128, 512], F32))
        mk_cap = eb_(nc.sbuf_tensor("mk_cap", [128, 512], F32))
        ps_cap2 = eb_(nc.sbuf_tensor("ps_cap2", [128, 512], F32))

    t_resid0_l = [sc.add(("sync", "gpsimd", "scalar")[si],
        (lambda t0, t1: lambda s: s.dma_start(
            out=resid[:, t0:t1, :],
            in_=xmine_fm[:, t0 * HALF:t1 * HALF].rearrange(
                "p (t n) -> p t n", n=HALF)))(lo, hi),
        waits=junction, dma=True)
        for si, (lo, hi) in enumerate(((0, 3), (3, 6), (6, 8)))]

    scopeB1 = ExitStack()
    e1 = scopeB1.enter_context
    o_my = e1(nc.sbuf_tensor("o_my", [128, 8, HALF], BF16))
    ogt = [e1(nc.sbuf_tensor(f"ogt{i}", [128, 2, HALF], BF16)) for i in range(2)]
    wo_bf = e1(nc.sbuf_tensor("wo_bf", [128, 8, 1024], BF16))
    wos = [e1(nc.sbuf_tensor(f"wos{i}", [128, 1024], F32)) for i in range(2)]

    two_last = [None, None]
    tw_o = []
    for i in ([] if skip_attn else range(8)):
        b = i % 2
        t_l = sc.add(("sync", "gpsimd", "scalar")[i % 3],
            (lambda bb, ii: lambda s: s.dma_start(
                out=wos[bb][:, :], in_=wo_in[ii * 128:(ii + 1) * 128, :]))(b, i),
            waits=[two_last[b]] + junction, dma=True)
        t_c = sc.add("vector", (lambda bb, ii: lambda v: v.tensor_copy(
            out=wo_bf[:, ii, :], in_=wos[bb][:, :]))(b, i), waits=[t_l] + junction)
        two_last[b] = t_c
        tw_o.append(t_c)

    t_omy = []
    prev_og = [None, None]
    for it in ([] if skip_attn else range(8)):
        b = it % 2
        t_g = sc.add(("sync", "gpsimd", "scalar")[it % 3],
            (lambda ii, bb: lambda s: s.dma_start(
                out=ogt[bb][:, :, :],
                in_=og_out_c[ii % 4].ap()[
                    (ii // 4) * 128:(ii // 4) * 128 + 128, :].rearrange(
                    "p (a n) -> p a n", a=2)))(it, b),
            waits=[t_og_chunks[it % 4] if t_og_chunks else None,
                   prev_og[b]] + junction, dma=True)
        t_ca = sc.add("vector", (lambda ii, bb: lambda v: v.tensor_scalar(
            out=o_my[:, ii, :], in0=ogt[bb][:, 0, :],
            scalar1=gA_sb[:, 0:1], scalar2=None, op0=ALU.mult))(it, b),
            waits=[t_g] + junction)
        t_c = sc.add("vector", (lambda ii, bb: lambda v: v.scalar_tensor_tensor(
            out=o_my[:, ii, :], in0=ogt[bb][:, 1, :],
            scalar=gB_sb[:, 0:1], in1=o_my[:, ii, :],
            op0=ALU.mult, op1=ALU.add))(it, b), hard=[t_ca])
        prev_og[b] = t_c
        t_omy.append(t_c)

    t_ffn_in = []
    evac_hist = []
    prev_mmo = [None, None]
    for ct in ([] if skip_attn else range(8)):
        bs = 4 * (ct % 2)
        t_mmp = sc.add("tensor", (lambda cc, bss: lambda t: [
            (mark_ldw if t4 > 0 else (lambda z: z))(
                t.matmul(ps[bss + t4][:, :], wo_bf[:, i2, cc * 128:(cc + 1) * 128],
                         o_my[:, i2, t4 * 512:(t4 + 1) * 512],
                         start=(i2 == 0), stop=(i2 == 7),
                         skip_group_check=True))
            for i2 in range(8) for t4 in range(4)][-1])(ct, bs),
            waits=t_omy + tw_o + [prev_mmo[ct % 2],
                                  evac_hist[ct - 2] if ct >= 2 else None])
        prev_mmo[ct % 2] = t_mmp
        t_last_ct = None
        for tc in range(4):
            t_ev = sc.add("vector", (lambda cc, t4, bss: lambda v: v.tensor_tensor(
                out=resid[:, cc, t4 * 512:(t4 + 1) * 512],
                in0=resid[:, cc, t4 * 512:(t4 + 1) * 512],
                in1=ps[bss + t4][:, :], op=ALU.add))(ct, tc, bs),
                waits=[t_mmp] + t_resid0_l)
            t_last_ct = t_ev
        evac_hist.append(t_last_ct)
        t_ffn_in.append(t_last_ct)

    if skip_attn:
        t_ffn_in = t_resid0_l
    scopeB1.close()

    if "resid1" in dbg:
        tr1 = dout("dbg_resid1", [128, 8, HALF], F32)
        t_ffn_in = t_ffn_in + [sc.add("gpsimd", lambda g: g.dma_start(
            out=tr1, in_=resid[:, :, :]), waits=t_ffn_in[:], dma=True)]

    def emit_ln_quant(t_ready, xq):
        toks = []
        t_prev = None
        for ch in range(4):
            c0 = ch * 512

            def dstq(v, d, bmu, br, _c0=c0):
                return [
                    lambda v2: v2.tensor_tensor(out=xc_s[:, :],
                                                in0=resid[:, d, _c0:_c0 + 512],
                                                in1=ps[bmu][:, :], op=ALU.subtract),
                    lambda v2: v2.tensor_tensor(out=xf_s[:, :], in0=xc_s[:, :],
                                                in1=ps[br][:, :], op=ALU.mult),
                    lambda v2: v2.tensor_copy(out=xq[:, d, _c0:_c0 + 512],
                                              in_=xf_s[:, :]),
                ]
            t_norm = emit_ln(512, (lambda _c0=c0: lambda d: resid[:, d, _c0:_c0 + 512])(),
                             dstq, t_ready, ch % 2)
            toks.extend(t_norm)
        return toks

    def emit_ffn(t_xq_toks, bufs, early_tap=False):
        xq, h_buf, w1s, w2s, h16, mk16 = bufs
        t_h = []
        prev_w1 = [None, None]
        prev_mm = [None, None]
        act_hist = []
        prev_cp = [None, None, None, None]
        pend = [None]
        last_act = [None]
        for ht2 in range(32):
            b = ht2 % 2
            bs = 4 * (ht2 % 2)
            t_w = [sc.add(("sync", "gpsimd")[wi % 2],
                (lambda hh, bb, t0, t1: lambda s: s.dma_start(
                    out=w1s[bb][:, t0:t1, :],
                    in_=w1q_d[:, hh * 1024 + t0 * 128:hh * 1024 + t1 * 128]
                    .rearrange("p (t n) -> p t n", n=128)))(ht2, b, 4 * wi, 4 * wi + 4),
                waits=[prev_w1[b], t_xq_toks[0]], dma=True)
                for wi in range(2)]
            t_mmp = sc.add("tensor", (lambda hh, bb, bss: lambda t: [
                t.matmul(ps[bss + t4][:, :], w1s[bb][:, 2 * d2:2 * d2 + 2, :],
                         xq[:, 2 * d2:2 * d2 + 2, t4 * 512:(t4 + 1) * 512],
                         start=(d2 == 0), stop=(d2 == 3),
                         skip_group_check=True, perf_mode=PM.DoubleRow)
                for d2 in range(4) for t4 in range(4)][-1])(ht2, b, bs),
                waits=t_xq_toks + t_w + [act_hist[ht2 - 2] if ht2 >= 2 else None,
                                         prev_mm[b]])
            prev_w1[b] = t_mmp
            prev_mm[b] = t_mmp
            for tc in range(4):
                t_a = sc.add("scalar", (lambda hh, bss, t4: lambda s: s.activation(
                    out=h16[t4][:, :], in_=ps[bss + t4][:, :], func=AF.Relu,
                    bias=b1q[:, hh:hh + 1]))(ht2, bs, tc),
                    waits=[t_mmp, prev_cp[tc]])
                if pend[0] is not None:
                    p_hh, p_t4 = pend[0]
                    t_c = sc.add("vector", (lambda hh, t4: lambda v: v.tensor_copy(
                        out=h_buf[:, hh, t4 * 512:(t4 + 1) * 512],
                        in_=h16[t4][:, :]))(p_hh, p_t4), waits=[t_a])
                    prev_cp[p_t4] = t_c
                    t_h.append(t_c)
                pend[0] = (ht2, tc)
                last_act[0] = t_a
            act_hist.append(last_act[0])
        if pend[0] is not None:
            p_hh, p_t4 = pend[0]
            t_flush = sc.add("scalar", lambda s: s.activation(
                out=varrow[:, 0:8], in_=varrow[:, 0:8], func=AF.Square),
                waits=[t_h[-1] if t_h else None, last_act[0]])
            t_c = sc.add("vector", (lambda hh, t4: lambda v: v.tensor_copy(
                out=h_buf[:, hh, t4 * 512:(t4 + 1) * 512],
                in_=h16[t4][:, :]))(p_hh, p_t4), waits=[t_flush])
            t_h.append(t_c)
        if early_tap:
            the = dout("dbg_hbuf_early", [128, 32, HALF], FP8)
            t_he = sc.add("gpsimd", lambda g: g.dma_start(
                out=the, in_=h_buf[:, :, :]), waits=[t_h[-1]], dma=True)
            t_h.append(t_he)
        t_out = []
        prev_w2 = [None, None]
        prev_mm2 = [None, None]
        act2_hist = []
        prev_mk = [None, None, None, None]
        prev_y16 = [None, None, None, None]
        prev_md = [None]
        prev_md2 = [None, None]
        for dt2 in range(8):
            b = dt2 % 2
            bs = 4 * (dt2 % 2)
            t_w = [sc.add(("sync", "gpsimd", "scalar", "sync")[wi],
                (lambda dd, bb, t0, t1: lambda s: s.dma_start(
                    out=w2s[bb][:, t0:t1, :],
                    in_=w2q_d[:, dd * 4096 + t0 * 128:dd * 4096 + t1 * 128]
                    .rearrange("p (t n) -> p t n", n=128)))(dt2, b, 8 * wi, 8 * wi + 8),
                waits=[prev_w2[b], t_h[-1]], dma=True)
                for wi in range(4)]
            t_mmp = sc.add("tensor", (lambda dd, bb, bss: lambda t: [
                t.matmul(ps[bss + t4][:, :], w2s[bb][:, 2 * h4:2 * h4 + 2, :],
                         h_buf[:, 2 * h4:2 * h4 + 2, t4 * 512:(t4 + 1) * 512],
                         start=(h4 == 0), stop=(h4 == 15),
                         skip_group_check=True, perf_mode=PM.DoubleRow)
                for h4 in range(16) for t4 in range(4)][-1])(dt2, b, bs),
                waits=t_w + [t_h[-1], prev_mm2[b],
                             act2_hist[dt2 - 2] if dt2 >= 2 else None])
            prev_w2[b] = t_mmp
            prev_mm2[b] = t_mmp
            t_last_dt = None
            t_a = None
            for tc in range(4):
                mb = tc
                t_mk = sc.add(("sync", "gpsimd", "scalar")[(dt2 * 4 + tc) % 3],
                    (lambda dd, t4, bb: lambda s: s.dma_start(
                        out=mk16[bb][:, :],
                        in_=dmask16[dd * 128:(dd + 1) * 128, t4 * 512:(t4 + 1) * 512]))(
                        dt2, tc, tc), waits=[prev_mk[tc], t_xq_toks[0]], dma=True)
                t_a = sc.add("scalar", (lambda dd, bss, t4: lambda s: s.activation(
                    out=h16[t4][:, :], in_=ps[bss + t4][:, :], func=AF.Identity,
                    bias=b2q[:, dd:dd + 1]))(dt2, bs, tc),
                    waits=[t_mmp, prev_y16[tc]])
                t_mda = sc.add("gpsimd", (lambda t4, bb, sb_: lambda g:
                    g.tensor_tensor(out=md2_s[sb_][:, :], in0=h16[t4][:, :],
                                    in1=mk16[bb][:, :], op=ALU.mult))(tc, mb, tc % 2),
                    waits=[t_a, t_mk, prev_md2[tc % 2]])
                t_md = sc.add("vector", (lambda dd, t4, sb_: lambda v:
                    v.tensor_tensor(out=resid[:, dd, t4 * 512:(t4 + 1) * 512],
                                    in0=resid[:, dd, t4 * 512:(t4 + 1) * 512],
                                    in1=md2_s[sb_][:, :], op=ALU.add))(dt2, tc, tc % 2),
                    waits=[t_mda, prev_md[0]])
                prev_md[0] = t_md
                prev_md2[tc % 2] = t_md
                prev_y16[tc] = t_md
                prev_mk[tc] = t_md
                t_last_dt = t_md
            act2_hist.append(t_a)
            t_out.append(t_last_dt)
        return t_out

    ffn_no = [0]

    def ffn_phase(t_ready):
        ffn_no[0] += 1
        fn_ = ffn_no[0]
        sp = ExitStack()
        ef = sp.enter_context
        xq = ef(nc.sbuf_tensor(f"xq_{fn_}", [128, 8, HALF], FP8))
        h_buf = ef(nc.sbuf_tensor(f"h_buf_{fn_}", [128, 32, HALF], FP8))
        w1s = [ef(nc.sbuf_tensor(f"w1s_{fn_}_{i}", [128, 8, 128], FP8)) for i in range(2)]
        w2s = [ef(nc.sbuf_tensor(f"w2s_{fn_}_{i}", [128, 32, 128], FP8)) for i in range(2)]
        h16 = [ef(nc.sbuf_tensor(f"h16_{fn_}_{i}", [128, 512], F16)) for i in range(4)]
        mk16 = [ef(nc.sbuf_tensor(f"mk16_{fn_}_{i}", [128, 512], FP8)) for i in range(4)]
        t_xq = emit_ln_quant(t_ready, xq)
        if "ffn0" in dbg and fn_ == 1:
            txq = dout("dbg_xq", [128, 8, HALF], FP8)
            t_tap = sc.add("gpsimd", lambda g: g.dma_start(
                out=txq, in_=xq[:, :, :]), waits=t_xq, dma=True)
            t_xq = t_xq + [t_tap]
        t_o = emit_ffn(t_xq, (xq, h_buf, w1s, w2s, h16, mk16),
                       early_tap=("ffn0" in dbg and fn_ == 1))
        if "ffn0" in dbg and fn_ == 1:
            th = dout("dbg_hbuf", [128, 32, HALF], FP8)
            t_tap2 = sc.add("gpsimd", lambda g: g.dma_start(
                out=th, in_=h_buf[:, :, :]), waits=t_o, dma=True)
            t_o = t_o + [t_tap2]
        sp.close()
        return t_o

    if "w1q" in dbg:
        tw1 = dout("dbg_w1q", [128, HID], FP8)
        tw2 = dout("dbg_w2q", [128, 2048], FP8)
        w1chk = eb_(nc.sbuf_tensor("w1chk", [128, HID], FP8))
        w2chk = eb_(nc.sbuf_tensor("w2chk", [128, 2048], FP8))
        t_wl = sc.add("sync", lambda s: s.dma_start(
            out=w1chk[:, :], in_=w1q_d[0:128, :]), waits=junction, dma=True)
        t_wl2 = sc.add("sync", lambda s: s.dma_start(
            out=w2chk[:, 0:1024], in_=w2q_d[0:128, :]), waits=junction, dma=True)
        t_wl3 = sc.add("sync", lambda s: s.dma_start(
            out=w2chk[:, 1024:2048], in_=w2q_d[3968:4096, :]), waits=junction, dma=True)
        sc.add("gpsimd", lambda g: g.dma_start(out=tw1, in_=w1chk[:, :]),
               waits=[t_wl], dma=True)
        sc.add("gpsimd", lambda g: g.dma_start(out=tw2, in_=w2chk[:, :]),
               waits=[t_wl2, t_wl3], dma=True)
    t_ffn1 = ffn_phase(t_ffn_in)

    if "resid2" in dbg:
        tr2 = dout("dbg_resid2", [128, 8, HALF], F32)
        t_ffn1 = t_ffn1 + [sc.add("gpsimd", lambda g: g.dma_start(
            out=tr2, in_=resid[:, :, :]), waits=t_ffn1[:], dma=True)]

    if stop_after == "ffn1":
        t_fin = sc.add("gpsimd", lambda g: g.dma_start(
            out=y_out.rearrange("(t p) n -> p t n", p=128), in_=resid[:, :, :]),
            waits=t_ffn1, dma=True)
        sc.add("gpsimd", lambda g: g.engine_nop(), waits=[t_fin])
        scopeB.close()
        block = root.enter_context(nc.Block())
        sc.emit(nc, block, root)
        apply_ldw_skip()
        root.close()
        return nc

    scopeS = ExitStack()
    es = scopeS.enter_context
    u_bf = es(nc.sbuf_tensor("u_bf", [128, 8, HALF], BF16))
    bu = es(nc.sbuf_tensor("bu", [128, 2, HALF], F32))
    lamb = es(nc.sbuf_tensor("lamb", [128, 2, HALF], F32))
    hsc = es(nc.sbuf_tensor("hsc", [128, 2, HALF], F32))
    h_bf = es(nc.sbuf_tensor("h_bf", [128, 2, HALF], BF16))
    onesw = es(nc.sbuf_tensor("onesw", [128, HALF], F32))
    carry_sb = es(nc.sbuf_tensor("carry_sb", [128, 2], F32))
    cin_sb = es(nc.sbuf_tensor("cin_sb", [128, 2], F32))

    t_ln3 = []
    t_prev = None
    for ch in range(4):
        c0 = ch * 512

        def dst3(v, d, bmu, br, _c0=c0):
            return [
                lambda v2: v2.tensor_tensor(out=xc_s[:, :],
                                            in0=resid[:, d, _c0:_c0 + 512],
                                            in1=ps[bmu][:, :], op=ALU.subtract),
                lambda v2: v2.tensor_tensor(out=xf_s[:, :], in0=xc_s[:, :],
                                            in1=ps[br][:, :], op=ALU.mult),
                lambda v2: last(
                    v2.tensor_copy(out=u_bf[:, d, _c0:_c0 + 512], in_=xf_s[:, :]),
                    v2.tensor_scalar(out=md_s[:, :], in0=xf_s[:, :],
                                     scalar1=d_sb[:, d:d + 1], scalar2=None,
                                     op0=ALU.mult)),
                lambda v2: v2.tensor_tensor(out=resid[:, d, _c0:_c0 + 512],
                                            in0=resid[:, d, _c0:_c0 + 512],
                                            in1=md_s[:, :], op=ALU.add),
            ]
        t_norm = emit_ln(512, (lambda _c0=c0: lambda d: resid[:, d, _c0:_c0 + 512])(),
                         dst3, t_ffn1, ch % 2)
        t_ln3.extend(t_norm)

    t_bu = []
    last_bank = {}
    for nt in range(2):
        for tc in range(4):
            bank = (nt * 4 + tc) % 4
            t_mmp = sc.add("tensor", (lambda nn, t4, bk: lambda t: [
                t.matmul(ps[bk][:, :], bwt_bf[:, d, nn * 128:(nn + 1) * 128],
                         u_bf[:, d, t4 * 512:(t4 + 1) * 512],
                         start=(d == 0), stop=(d == 7))
                for d in range(8)][-1])(nt, tc, bank),
                waits=t_ln3 + [last_bank.get(bank)])
            t_ev = sc.add("vector", (lambda nn, t4, bk: lambda v: v.tensor_copy(
                out=bu[:, nn, t4 * 512:(t4 + 1) * 512], in_=ps[bk][:, :]))(
                    nt, tc, bank), waits=[t_mmp])
            last_bank[bank] = t_ev
            t_bu.append(t_ev)

    t_lb0 = sc.add("vector", lambda v: v.memset(onesw[:, :], 1.0),
                   waits=[t_lam] + t_ffn1)
    t_lb = sc.add("vector", lambda v: last(
        v.tensor_scalar(out=lamb[:, 0, :], in0=onesw[:, :],
                        scalar1=lam_col[:, 0:1], scalar2=None, op0=ALU.mult),
        v.tensor_scalar(out=lamb[:, 1, :], in0=onesw[:, :],
                        scalar1=lam_col[:, 1:2], scalar2=None, op0=ALU.mult),
    ), hard=[t_lb0])

    t_s1a = sc.add("vector", lambda v: last(
        v.tensor_tensor_scan(out=hsc[:, 0, :], data0=lamb[:, 0, :],
                             data1=bu[:, 0, :], initial=0.0,
                             op0=ALU.mult, op1=ALU.add),
        v.tensor_tensor_scan(out=hsc[:, 1, :], data0=lamb[:, 1, :],
                             data1=bu[:, 1, :], initial=0.0,
                             op0=ALU.mult, op1=ALU.add),
    ), waits=[t_lb] + t_bu)
    t_s1 = sc.add("vector", lambda v: last(
        v.tensor_scalar(out=carry_sb[:, 0:1], in0=hsc[:, 0, HALF - 1:HALF],
                        scalar1=gA_sb[:, 0:1], scalar2=None, op0=ALU.mult),
        v.tensor_scalar(out=carry_sb[:, 1:2], in0=hsc[:, 1, HALF - 1:HALF],
                        scalar1=gA_sb[:, 0:1], scalar2=None, op0=ALU.mult),
    ), hard=[t_s1a])
    t_cst = sc.add("gpsimd", lambda g: g.dma_start(
        out=carry_in_d.ap(), in_=carry_sb[:, :]), waits=[t_s1], dma=True)
    t_cc2 = sc.add("gpsimd", lambda g: g.collective_compute(
        "AllGather", ALU.bypass,
        replica_groups=[[0, 1], [2, 3], [4, 5], [6, 7]],
        ins=[carry_in_d[:, :]], outs=[carry_out_d[:, :]]),
        waits=[t_cst], own_sem=True)
    t_cld = sc.add("gpsimd", lambda g: g.dma_start(
        out=cin_sb[:, :], in_=carry_out_d.ap()[0:128, :]),
        waits=[t_cc2], dma=True)

    t_cing = sc.add("vector", lambda v: v.tensor_scalar(
        out=cin_sb[:, :], in0=cin_sb[:, :],
        scalar1=gB_sb[:, 0:1], scalar2=None, op0=ALU.mult),
        waits=[t_cld, t_s1])
    t_s2a = sc.add("vector", lambda v: last(
        v.tensor_tensor_scan(out=hsc[:, 0, :], data0=lamb[:, 0, :],
                             data1=bu[:, 0, :], initial=cin_sb[:, 0:1],
                             op0=ALU.mult, op1=ALU.add),
        v.tensor_tensor_scan(out=hsc[:, 1, :], data0=lamb[:, 1, :],
                             data1=bu[:, 1, :], initial=cin_sb[:, 1:2],
                             op0=ALU.mult, op1=ALU.add),
    ), hard=[t_cing])
    t_s2 = sc.add("vector", lambda v: last(
        v.tensor_copy(out=h_bf[:, 0, :], in_=hsc[:, 0, :]),
        v.tensor_copy(out=h_bf[:, 1, :], in_=hsc[:, 1, :]),
    ), hard=[t_s2a])

    t_cp = []
    last_bank = {}
    for dt2 in range(8):
        t_last_dt = None
        for tc in range(4):
            bank = (dt2 * 4 + tc) % 4
            t_mmp = sc.add("tensor", (lambda dd, t4, bk: lambda t: [
                t.matmul(ps[bk][:, :], cwt_bf[:, nn, dd * 128:(dd + 1) * 128],
                         h_bf[:, nn, t4 * 512:(t4 + 1) * 512],
                         start=(nn == 0), stop=(nn == 1))
                for nn in range(2)][-1])(dt2, tc, bank),
                waits=[t_s2, last_bank.get(bank)])
            t_ev = sc.add("vector", (lambda dd, t4, bk: lambda v: v.tensor_tensor(
                out=resid[:, dd, t4 * 512:(t4 + 1) * 512],
                in0=resid[:, dd, t4 * 512:(t4 + 1) * 512],
                in1=ps[bk][:, :], op=ALU.add))(dt2, tc, bank),
                waits=[t_mmp])
            last_bank[bank] = t_ev
            t_last_dt = t_ev
        t_cp.append(t_last_dt)

    scopeS.close()

    if "resid3" in dbg:
        tr3 = dout("dbg_resid3", [128, 8, HALF], F32)
        t_cp = t_cp + [sc.add("gpsimd", lambda g: g.dma_start(
            out=tr3, in_=resid[:, :, :]), waits=t_cp[:], dma=True)]

    t_ffn2 = ffn_phase(t_cp)

    t_yo = [sc.add(("sync", "gpsimd")[dt2 % 2],
        (lambda dd: lambda s: s.dma_start(
            out=y_out[dd * 128:(dd + 1) * 128, :], in_=resid[:, dd, :]))(dt2),
        waits=[t_ffn2[dt2] if dt2 < len(t_ffn2) else t_ffn2[-1]], dma=True)
        for dt2 in range(8)]
    sc.add("gpsimd", lambda g: g.engine_nop(), waits=t_yo)

    scopeB.close()
    block = root.enter_context(nc.Block())
    sc.emit(nc, block, root)
    apply_ldw_skip()
    root.close()
    return nc



def _np_keep_mask():
    def threefry2x32(k1, k2, x1, x2):
        def rotl(x, d):
            return ((x << np.uint32(d)) | (x >> np.uint32(32 - d))).astype(np.uint32)
        ks0, ks1 = np.uint32(k1), np.uint32(k2)
        ks2 = np.uint32(ks0 ^ ks1 ^ np.uint32(0x1BD11BDA))
        x1 = (x1 + ks0).astype(np.uint32)
        x2 = (x2 + ks1).astype(np.uint32)
        rotations = [(13, 15, 26, 6), (17, 29, 16, 24)]
        ks = [ks0, ks1, ks2]
        for i in range(5):
            for rot in rotations[i % 2]:
                x1 = (x1 + x2).astype(np.uint32)
                x2 = rotl(x2, rot)
                x2 = (x2 ^ x1).astype(np.uint32)
            x1 = (x1 + ks[(i + 1) % 3]).astype(np.uint32)
            x2 = (x2 + ks[(i + 2) % 3] + np.uint32(i + 1)).astype(np.uint32)
        return x1, x2

    shape = (4, SEQ, D_MODEL)
    n = int(np.prod(shape))
    idx = np.arange(n, dtype=np.uint64)
    hi = (idx >> np.uint64(32)).astype(np.uint32)
    lo = (idx & np.uint64(0xFFFFFFFF)).astype(np.uint32)
    with np.errstate(over="ignore"):
        b1, b2 = threefry2x32(0, 42, hi, lo)
    bits = b1 ^ b2
    f = ((bits >> np.uint32(9)) | np.uint32(0x3F800000)).view(np.float32) - np.float32(1.0)
    return (f < np.float32(0.8)).reshape(shape)


def make_inputs(kw):
    x = np.asarray(kw["L0_attn_a_x"], np.float32)
    Q = np.asarray(kw["L0_attn_a_Q"], np.float32)
    K = np.asarray(kw["L0_attn_a_K"], np.float32)
    V = np.asarray(kw["L0_attn_a_V"], np.float32)
    O = np.asarray(kw["L0_attn_a_O"], np.float32)
    ll = np.asarray(kw["L0_attn_b_log_lambda"], np.float32)
    Bw = np.asarray(kw["L0_attn_b_B_w"], np.float32)
    Cw = np.asarray(kw["L0_attn_b_C_w"], np.float32)
    D = np.asarray(kw["L0_attn_b_D"], np.float32)
    w1 = np.asarray(kw["L0_ffn_w1"], np.float32)
    b1 = np.asarray(kw["L0_ffn_b1"], np.float32)
    w2 = np.asarray(kw["L0_ffn_w2"], np.float32)
    b2 = np.asarray(kw["L0_ffn_b2"], np.float32)

    keep = _np_keep_mask()
    mask16 = (keep.astype(np.float32) * np.float32(1.25)).astype(ml_dtypes.float8_e4m3)

    tril = np.tril(np.ones((128, 128), np.float32))
    masknegqk = ((1.0 - tril) * np.float32(NEG_BIG)).astype(np.float32)
    keepkq = np.triu(np.ones((128, 128), np.float32)).astype(ml_dtypes.bfloat16)
    mnegkq = ((1.0 - np.triu(np.ones((128, 128), np.float32))) *
              np.float32(NEG_BIG)).astype(np.float32)
    mneg4 = np.hstack([
        mnegkq, np.zeros((128, 128), np.float32),
        np.full((128, 128), NEG_BIG, np.float32), mnegkq,
    ]).astype(ml_dtypes.bfloat16)
    ident = np.eye(128, dtype=np.float32)

    w1t = np.ascontiguousarray(
        w1.T.reshape(8, 128, 32, 128).transpose(1, 2, 0, 3).reshape(128, 32768))
    w2t = np.ascontiguousarray(
        w2.T.reshape(32, 128, 8, 128).transpose(1, 2, 0, 3).reshape(128, 32768))
    bwt = np.ascontiguousarray(Bw.T)
    cwt = np.ascontiguousarray(Cw.T)
    lamlog = np.ascontiguousarray(ll.reshape(2, 128).T)

    in_maps = []
    for c in range(8):
        b, half = c // 2, c % 2
        wqk = np.empty((1024, 1024), np.float32)
        wvl = np.empty((1024, 512), np.float32)
        for j in range(8):
            h = 8 * half + j
            wqk[:, j * 128:j * 128 + 64] = Q[:, h, :]
            wqk[:, j * 128 + 64:j * 128 + 128] = K[:, h, :]
            wvl[:, j * 64:(j + 1) * 64] = V[:, h, :]
        sl = slice(HALF * half, HALF * half + HALF)
        im = dict(
            x_fm=np.ascontiguousarray(x[b].T),
            xmine_fm=np.ascontiguousarray(
                x[b, sl].T.reshape(8, 128, HALF).transpose(1, 0, 2)
                .reshape(128, 8 * HALF)),
            wqk=wqk, wv=wvl, wo=O,
            w1t=w1t, w2t=w2t,
            b1c=np.ascontiguousarray(b1.reshape(32, 128).T),
            b2c=np.ascontiguousarray(b2.reshape(8, 128).T),
            bwt=bwt, cwt=cwt, lamlog=lamlog,
            dcol=np.ascontiguousarray(D.reshape(8, 128).T),
            dmask16=np.ascontiguousarray(mask16[b, sl].T),
            masknegqk=masknegqk, keepkq=keepkq, mnegkq=mnegkq, ident=ident,
            mneg4=mneg4,
            gA=np.full((128, 1), 1.0 - half, np.float32),
            gB=np.full((128, 1), float(half), np.float32),
        )
        in_maps.append(im)
    return in_maps


def kernel(**kw):
    in_maps = make_inputs(kw)
    nc = build_program()
    res = run_bass_kernel_spmd(nc, in_maps, core_ids=list(range(8)))
    out = np.empty((4, SEQ, D_MODEL), np.float32)
    for c in range(8):
        b, half = c // 2, c % 2
        out[b, HALF * half:HALF * half + HALF, :] = res.results[c]["y_out"].T
    return out
